# version 09
# baseline (speedup 1.0000x reference)
import math
import contextlib
from dataclasses import dataclass

import numpy as np
import ml_dtypes

import concourse.bass as bass
import concourse.mybir as mybir

F32 = mybir.dt.float32
BF16 = mybir.dt.bfloat16
AF = mybir.ActivationFunctionType
ALU = mybir.AluOpType
BFD = ml_dtypes.bfloat16

NEG = -1.0e12
PW = 512
EPS = 1e-5
ISQ = 1.0 / math.sqrt(128.0)
SQ = math.sqrt(128.0)


@dataclass
class CFG:
    T: int = 4096
    D: int = 4096
    H: int = 16
    TCH: int = 1024
    TCW: int = 512
    QCPW: int = 1024

    @property
    def HT(self): return 2 * self.H
    @property
    def F(self): return 3 * self.H * 128
    @property
    def FC(self): return self.F // 128
    @property
    def DC(self): return self.D // 128
    @property
    def KT(self): return self.T // 128
    @property
    def OWN(self): return self.T // 2
    @property
    def NCH(self): return self.T // self.TCH
    @property
    def NTC(self): return self.TCH // self.TCW
    @property
    def NQCP(self): return self.T // self.QCPW
    @property
    def NTW(self): return self.QCPW // self.TCW
    @property
    def NOC(self): return self.OWN // self.TCW


def alibi_slopes(n_heads, alibi_bias_max=8):
    _n = 2 ** math.ceil(math.log2(n_heads))
    m = np.arange(1, _n + 1, dtype=np.float64) * (alibi_bias_max / _n)
    slopes = 1.0 / np.power(2.0, m)
    if _n != n_heads:
        slopes = np.concatenate([slopes[1::2], slopes[::2]])[:n_heads]
    return slopes.astype(np.float32)



def host_prepare(cfg, x, w_qkv, b_qkv, w_out, b_out, W, bffn, A, Bf, coeffs, scales):
    c = cfg
    T, D, H = c.T, c.D, c.H
    slopes = alibi_slopes(c.HT)

    def blocked(mT):
        Dd, N = mT.shape
        return np.ascontiguousarray(
            mT.reshape(Dd // 128, 128, N // 128, 128).transpose(2, 1, 0, 3))

    W = np.asarray(W); w_qkv = np.asarray(w_qkv); w_out = np.asarray(w_out)
    A = np.asarray(A); Bf = np.asarray(Bf)
    WT_blk = blocked(np.ascontiguousarray(W.T))
    A_flatT = np.ascontiguousarray(A.reshape(256, D).T)
    A_T3 = np.ascontiguousarray(A_flatT.reshape(c.DC, 128, 256).transpose(1, 0, 2))
    Bf_er = np.ascontiguousarray(Bf.transpose(0, 2, 1).reshape(256, D))

    ident = np.eye(128, dtype=np.float32)
    NP = c.TCW // 128
    tri = np.zeros((max(NP, 1), 128, c.TCW), dtype=np.float32)
    for pp in range(NP):
        for k in range(128):
            tri[pp, k, : pp * 128 + k] = NEG
    E0 = np.zeros((128, 128), dtype=np.float32)
    E1 = np.zeros((128, 128), dtype=np.float32)
    for pp in range(128):
        E0[pp // 16, pp] = 1.0
        E1[8 + pp // 16, pp] = 1.0

    ins = []
    for core in range(8):
        b, me = core // 2, core % 2
        xb = np.ascontiguousarray(np.asarray(x)[b].T.astype(np.float32))
        rq = slice(H * 128 * me, H * 128 * (me + 1))
        rk = slice(D + H * 128 * me, D + H * 128 * (me + 1))
        rv = slice(2 * D + H * 128 * me, 2 * D + H * 128 * (me + 1))
        wq = np.concatenate([w_qkv[rq], w_qkv[rk], w_qkv[rv]], axis=0)
        wqkvT_blk = blocked(np.ascontiguousarray(wq.T))
        bq = np.concatenate([np.asarray(b_qkv)[rq], np.asarray(b_qkv)[rk],
                             np.asarray(b_qkv)[rv]]).astype(np.float32)
        woT = np.ascontiguousarray(w_out.T)
        order = np.concatenate([
            np.arange(H * 128 * me, H * 128 * (me + 1)),
            np.arange(H * 128 * (1 - me), H * 128 * (2 - me))])
        woutT_blk = blocked(np.ascontiguousarray(woT[order]))
        hs = slopes[H * me: H * me + H]
        al = np.zeros((128, H * c.KT), dtype=np.float32)
        for h in range(H):
            for kt in range(c.KT):
                al[:, h * c.KT + kt] = hs[h] * ((kt * 128 + np.arange(128)) - (T - 1))
        sh = np.zeros((H, 128, T), dtype=np.float32)
        qq = np.arange(T, dtype=np.float64)
        for h in range(H):
            sh[h, :, :] = (hs[h] * (T - 1 - qq) * SQ).astype(np.float32)[None, :]
        selT = np.zeros((128, 2), dtype=np.float32)
        selT[:, me] = 1.0
        selR = np.zeros((128, 4), dtype=np.float32)
        selR[:, 2 * me + (1 - me)] = 1.0
        d = dict(
            x_fm=xb,
            x_fm_own=np.ascontiguousarray(xb[:, c.OWN * me: c.OWN * (me + 1)]),
            wqkvT_blk=wqkvT_blk, bqkv=bq,
            woutT_blk=woutT_blk, bout=np.asarray(b_out).astype(np.float32),
            WT_blk=WT_blk, bffn=np.asarray(bffn).astype(np.float32),
            A_T3=A_T3, Bf_er=Bf_er,
            coeffs=np.asarray(coeffs).astype(np.float32),
            scales=np.asarray(scales).astype(np.float32),
            alibi_tbl=al, shift_tbl=sh, tri_neg=tri,
            E0=E0, E1=E1, ident_in=ident, selT=selT, selR=selR,
        )
        ins.append({k: np.ascontiguousarray(v) for k, v in d.items()})
    return ins



ENGN = ("tensor", "vector", "scalar", "sync", "gpsimd")
DMA_ENG = {"sync", "gpsimd"}


class Prog:
    def __init__(self, nc):
        self.nc = nc
        self.ops = {e: [] for e in ENGN}
        self.waited = {e: {} for e in ENGN}
        self.sems = {}
        for e in ENGN:
            self.sems[e] = [nc.alloc_semaphore(name=f"se_{e}"), 0]
        self.sems["cc"] = [nc.alloc_semaphore(name="se_cc"), 0]
        self.buf = {}

    def _sem_handle(self, name):
        return self.sems[name][0]

    def stream_sem(self, name):
        name = "ds_" + name
        if name not in self.sems:
            self.sems[name] = [self.nc.alloc_semaphore(name=name), 0]
        return name

    def op(self, eng, fn, reads=(), writes=(), extra=(), dma=False, cc=False,
           stream=None):
        waits = {}
        def add(sn, v):
            if v and waits.get(sn, 0) < v:
                waits[sn] = v
        for k in reads:
            b = self.buf.get(k)
            if b:
                for sn, v in b["w"].items():
                    add(sn, v)
        for k in writes:
            b = self.buf.get(k)
            if b:
                for sn, v in b["w"].items():
                    add(sn, v)
                for sn, v in b["r"].items():
                    add(sn, v)
        for sn, v in extra:
            add(sn, v)
        wl = []
        for sn, v in waits.items():
            if sn == eng == "tensor":
                continue
            if self.waited[eng].get(sn, -1) >= v:
                continue
            self.waited[eng][sn] = v
            wl.append((self._sem_handle(sn), v))
        if cc:
            sname = "cc"
        elif dma:
            assert stream is not None, "DMA ops need a stream name"
            sname = self.stream_sem(stream)
        else:
            sname = eng
        sem, val = self.sems[sname]
        k_inc = 16 if dma else 1
        val += k_inc
        self.sems[sname][1] = val
        mile = (sname, val)

        def run(e, wl=wl, fn=fn, sem=sem, k_inc=k_inc):
            for h, v in wl:
                e.wait_ge(h, v)
            ins = fn(e)
            ins.then_inc(sem, k_inc)
        self.ops[eng].append(run)
        for k in writes:
            b = self.buf.setdefault(k, {"w": {}, "r": {}})
            if mile[0] in b["w"] or (not dma and not cc):
                b["w"] = {mile[0]: mile[1]}
                b["r"] = {}
            else:
                b["w"][mile[0]] = mile[1]
                b["r"] = {}
        for k in reads:
            b = self.buf.setdefault(k, {"w": {}, "r": {}})
            if b["r"].get(mile[0], 0) < mile[1]:
                b["r"][mile[0]] = mile[1]
        return mile

    def wait_only(self, eng, extra):
        wl = []
        for sn, v in extra:
            if self.waited[eng].get(sn, -1) >= v:
                continue
            self.waited[eng][sn] = v
            wl.append((self._sem_handle(sn), v))
        if wl:
            def run(e, wl=wl):
                for h, v in wl:
                    e.wait_ge(h, v)
            self.ops[eng].append(run)

    def dma_fence(self):
        vals = {sn: v[1] for sn, v in self.sems.items() if sn not in ENGN}
        for e in ENGN:
            self.wait_only(e, [(sn, v) for sn, v in vals.items() if v])

    def barrier(self, skip=()):
        vals = {sn: self.sems[sn][1] for sn in self.sems if sn not in skip}
        for e in ENGN:
            self.wait_only(e, [(sn, v) for sn, v in vals.items() if sn != e])
        marks = {}
        for e in ENGN:
            marks[e] = self.op(e, self.mark_fns[e],
                               reads=["barz"], writes=[f"bar_{e}"])
        for e in ENGN:
            self.wait_only(e, [(o, marks[o][1]) for o in ENGN if o != e])

    def replay(self):
        nc = self.nc
        with nc.Block() as block:
            @block.tensor
            def _(t):
                for f in self.ops["tensor"]:
                    f(t)

            @block.vector
            def _(v):
                for f in self.ops["vector"]:
                    f(v)

            @block.scalar
            def _(s):
                for f in self.ops["scalar"]:
                    f(s)

            @block.sync
            def _(sy):
                for f in self.ops["sync"]:
                    f(sy)

            @block.gpsimd
            def _(g):
                for f in self.ops["gpsimd"]:
                    f(g)


def _nop(e):
    try:
        return e.engine_nop()
    except AttributeError:
        return e.nop()


def build_nc(cfg, dbg=(), phases=("A", "B", "X", "C")):
    c = cfg
    nc = bass.Bass("TRN2", target_bir_lowering=False, num_devices=8)
    T, D, H, F = c.T, c.D, c.H, c.F
    DC, FC, KT, OWN, TCW = c.DC, c.FC, c.KT, c.OWN, c.TCW
    NP = TCW // 128

    di = lambda n, s: nc.dram_tensor(n, s, F32, kind="ExternalInput")
    x_fm = di("x_fm", [D, T])
    x_fm_own = di("x_fm_own", [D, OWN])
    wqkvT_blk = di("wqkvT_blk", [FC, 128, DC, 128])
    bqkv = di("bqkv", [F])
    woutT_blk = di("woutT_blk", [DC, 128, DC, 128])
    bout = di("bout", [D])
    WT_blk = di("WT_blk", [DC, 128, DC, 128])
    bffn = di("bffn", [D])
    A_T3 = di("A_T3", [128, DC, 256])
    Bf_er = di("Bf_er", [256, D])
    coeffs = di("coeffs", [16])
    scales = di("scales", [16])
    alibi_tbl = di("alibi_tbl", [128, H * KT])
    shift_tbl = di("shift_tbl", [H, 128, T])
    tri_neg_i = di("tri_neg", [max(NP, 1), 128, TCW])
    E0_i = di("E0", [128, 128])
    E1_i = di("E1", [128, 128])
    ident_i = di("ident_in", [128, 128])
    selT_i = di("selT", [128, 2])
    selR_i = di("selR", [128, 4])

    out_e = nc.dram_tensor("out_e", [OWN, D], F32, kind="ExternalOutput")
    qkv_d = nc.dram_tensor("qkv_d", [F, T], BF16, kind="Internal")
    T01_d = nc.dram_tensor("T01_d", [2, H * 128, OWN], BF16, kind="Internal")
    R01_d = nc.dram_tensor("R01_d", [2, 2, H * 128, OWN], BF16, kind="Internal")
    attn1_d = nc.dram_tensor("attn1_d", [D, OWN], BF16, kind="Internal")

    dbg_t = {}
    if "qkv" in dbg:
        dbg_t["qkv"] = nc.dram_tensor("dbg_qkv", [F, T], BF16, kind="ExternalOutput")
    if "attn" in dbg:
        dbg_t["attn0"] = nc.dram_tensor("dbg_attn0", [H * 128, OWN], BF16,
                                        kind="ExternalOutput")
        dbg_t["attn1"] = nc.dram_tensor("dbg_attn1", [H * 128, OWN], BF16,
                                        kind="ExternalOutput")
    if "x1" in dbg:
        dbg_t["x1"] = nc.dram_tensor("dbg_x1", [D, OWN], BF16, kind="ExternalOutput")

    p = Prog(nc)
    top = contextlib.ExitStack()
    SB = lambda n, s, d: top.enter_context(nc.sbuf_tensor(n, s, d))
    bar_sb = SB("bar_sb", [1, 8], F32)
    bar_bf = SB("bar_bf", [1, 2], BF16)
    p.mark_fns = {
        "tensor": lambda e: e.ldweights(bar_bf[0:1, 0:1]),
        "vector": lambda e: e.memset(bar_sb[0:1, 1:2], 0.0),
        "scalar": lambda e: e.copy(bar_sb[0:1, 2:3], bar_sb[0:1, 0:1]),
        "sync": _nop,
        "gpsimd": _nop,
    }

    ones_c_bf = SB("ones_c_bf", [128, 1], BF16)
    ones_c_f = SB("ones_c_f", [128, 1], F32)
    ones_r_f = SB("ones_r_f", [1, 128], F32)
    ones_row_bf = SB("ones_row_bf", [1, TCW], BF16)
    ident = SB("ident", [128, 128], F32)
    sel_sb = SB("sel_sb", [128, 6], F32)
    bout_sb = SB("bout_sb", [128, DC], F32)
    rowsS = SB("rowsS", [1, 4 * TCW], F32)
    bqb2 = SB("bqb2", [1, F], BF16)
    wcs_bf = SB("wcs_bf", [1, F], BF16)
    negmS = SB("negmS", [1, max(c.TCH, OWN)], BF16)
    rinv_bf = SB("rinv_bf", [1, c.TCH], BF16)
    rbcS = SB("rbcS", [128, max(c.TCH, OWN)], F32)
    acol_bf = SB("acol_bf", [1, 256], BF16)
    wexp_sb = SB("wexp_sb", [128, 2], F32)
    wcs2 = SB("wcs2", [1, 128], BF16)
    we_tmp = SB("we_tmp", [128, 4], F32)
    Epad = SB("Epad", [128, 256], F32)

    G, SY, V, S, TE = "gpsimd", "sync", "vector", "scalar", "tensor"

    p.op(G, lambda e: e.dma_start(out=ident[:, :], in_=ident_i[:, :]),
         writes=["ident"], dma=True, stream="init")
    p.op(G, lambda e: e.dma_start(out=sel_sb[:, 0:2], in_=selT_i[:, :]),
         writes=["selT"], dma=True, stream="init")
    p.op(G, lambda e: e.dma_start(out=sel_sb[:, 2:6], in_=selR_i[:, :]),
         writes=["selR"], dma=True, stream="init")
    for oc in range(DC):
        p.op(G, lambda e, oc=oc: e.dma_start(
            out=bout_sb[:, oc:oc + 1],
            in_=bout.ap()[oc * 128:(oc + 1) * 128].rearrange("(p one) -> p one", one=1)),
            writes=["boutc"], dma=True, stream="init")

    p.op(V, lambda e: e.memset(bar_sb[0:1, :], 0.0), writes=["barz"])
    p.op(V, lambda e: e.memset(bar_bf[0:1, :], 1.0), writes=["barz"])
    p.op(V, lambda e: e.memset(ones_c_bf[:, :], 1.0), writes=["ones_c"])
    p.op(V, lambda e: e.memset(ones_c_f[:, :], 1.0), writes=["ones_cf"])
    p.op(V, lambda e: e.memset(ones_r_f[:, :], 1.0), writes=["ones_r"])
    p.op(V, lambda e: e.memset(ones_row_bf[:, :], 1.0), writes=["ones_row"])

    p.barrier()

    with contextlib.ExitStack() as sa:
      if "A" in phases:
        SBa = lambda n, s, d: sa.enter_context(nc.sbuf_tensor(n, s, d))
        PSa = lambda n, s: sa.enter_context(nc.psum_tensor(n, s, F32))
        TCH, NTC, NCH = c.TCH, c.NTC, c.NCH
        xch = [SBa(f"xch{i}", [128, DC * TCH], BF16) for i in range(2)]
        wcol = [SBa(f"wcol{i}", [128, DC * 128], BF16) for i in range(2)]
        qo = [SBa(f"qo{i}", [128, TCH], BF16) for i in range(2)]
        sq4 = SBa("sq4", [128, 2 * TCH], BF16)
        rows = rowsS
        negm_bf = negmS
        rbc = rbcS
        bq_bf = bqb2
        pm = [PSa(f"pmA{i}", [128, NTC * PW]) for i in range(2)]
        prb = PSa("prbA", [128, NTC * PW])
        p.op(G, lambda e: e.dma_start(
            out=bq_bf[0:1, :], in_=bqkv.ap().rearrange("(one f) -> one f", one=1)),
            writes=["bq"], dma=True, stream="init")
        p.dma_fence()

        for ch in range(NCH):
            par = ch % 2
            XK = f"xch{par}"
            for dc in range(DC):
                p.op(G, lambda e, dc=dc, ch=ch, par=par: e.dma_start(
                    out=xch[par][:, dc * TCH:(dc + 1) * TCH],
                    in_=x_fm[dc * 128:(dc + 1) * 128, ch * TCH:(ch + 1) * TCH]),
                    writes=[XK], dma=True, stream=f"xc{par}")

            for grp in range(DC // 2):
                for j in range(2):
                    dc = grp * 2 + j
                    p.op(S, lambda e, dc=dc, j=j, par=par: e.activation(
                        sq4[:, j * TCH:(j + 1) * TCH],
                        xch[par][:, dc * TCH:(dc + 1) * TCH], AF.Square),
                        reads=[XK], writes=[f"sq{j}"])
                for j in range(2):
                    dc = grp * 2 + j
                    st, sp = (dc == 0), (dc == DC - 1)
                    for tc in range(NTC):
                        p.op(TE, lambda e, dc=dc, tc=tc, par=par, st=st, sp=sp: e.matmul(
                            pm[0][0:1, tc * PW: tc * PW + TCW], ones_c_bf[:, :],
                            xch[par][:, dc * TCH + tc * TCW: dc * TCH + (tc + 1) * TCW],
                            start=st, stop=sp),
                            reads=[XK, "ones_c"], writes=["pm0"])
                        p.op(TE, lambda e, j=j, tc=tc, st=st, sp=sp: e.matmul(
                            pm[1][0:1, tc * PW: tc * PW + TCW], ones_c_bf[:, :],
                            sq4[:, j * TCH + tc * TCW: j * TCH + (tc + 1) * TCW],
                            start=st, stop=sp),
                            reads=[f"sq{j}", "ones_c"], writes=["pm1"])
            R = rows
            for tc in range(NTC):
                o = tc * TCW
                p.op(V, lambda e, tc=tc: e.tensor_scalar(
                    R[0:1, 0:TCW], pm[0][0:1, tc * PW: tc * PW + TCW],
                    1.0 / D, None, ALU.mult), reads=["pm0"], writes=["rows"])
                p.op(V, lambda e, tc=tc: e.tensor_scalar(
                    R[0:1, TCW:2 * TCW], pm[1][0:1, tc * PW: tc * PW + TCW],
                    1.0 / D, None, ALU.mult), reads=["pm1"], writes=["rows"])
                p.op(V, lambda e: e.tensor_mul(
                    R[0:1, 2 * TCW:3 * TCW], R[0:1, 0:TCW], R[0:1, 0:TCW]),
                    reads=["rows"], writes=["rows"])
                p.op(V, lambda e: e.tensor_sub(
                    R[0:1, 2 * TCW:3 * TCW], R[0:1, TCW:2 * TCW], R[0:1, 2 * TCW:3 * TCW]),
                    reads=["rows"], writes=["rows"])
                p.op(V, lambda e: e.tensor_scalar(
                    R[0:1, 2 * TCW:3 * TCW], R[0:1, 2 * TCW:3 * TCW], EPS, None,
                    ALU.add), reads=["rows"], writes=["rows"])
                p.op(S, lambda e: e.activation(
                    R[0:1, TCW:2 * TCW], R[0:1, 2 * TCW:3 * TCW], AF.Ln),
                    reads=["rows"], writes=["rows"])
                p.op(S, lambda e: e.activation(
                    R[0:1, 3 * TCW:4 * TCW], R[0:1, TCW:2 * TCW], AF.Exp, scale=-0.5),
                    reads=["rows"], writes=["rows"])
                p.op(S, lambda e, o=o: e.activation(
                    rinv_bf[0:1, o:o + TCW], R[0:1, TCW:2 * TCW], AF.Exp, scale=0.5),
                    reads=["rows"], writes=["rinv"])
                p.op(V, lambda e, o=o: e.tensor_scalar(
                    negm_bf[0:1, o:o + TCW], R[0:1, 0:TCW], -1.0, None, ALU.mult),
                    reads=["rows"], writes=["negm"])
                p.op(TE, lambda e, tc=tc: e.matmul(
                    prb[:, tc * PW: tc * PW + TCW], ones_r_f[:, :],
                    R[0:1, 3 * TCW:4 * TCW], start=True, stop=True),
                    reads=["rows", "ones_r"], writes=["prb"])
                p.op(V, lambda e, o=o, tc=tc: e.tensor_copy(
                    rbc[:, o:o + TCW], prb[:, tc * PW: tc * PW + TCW]),
                    reads=["prb"], writes=["rbc"])

            for fc in range(FC):
                gfc = ch * FC + fc
                wpar = gfc % 2
                WK = f"wcol{wpar}"
                p.op(G, lambda e, fc=fc, wpar=wpar: e.dma_start(
                    out=wcol[wpar].ap().rearrange("p (dc f) -> p dc f", f=128),
                    in_=wqkvT_blk[fc, :, :, :]),
                    writes=[WK], dma=True, stream=f"wl{wpar}")
                if ch == 0:
                    for dc in range(DC):
                        p.op(TE, lambda e, dc=dc, wpar=wpar: e.matmul(
                            prb[0:1, 0:128], ones_c_bf[:, :],
                            wcol[wpar][:, dc * 128:(dc + 1) * 128],
                            start=(dc == 0), stop=(dc == DC - 1)),
                            reads=[WK, "ones_c"], writes=["prb"])
                    p.op(V, lambda e, fc=fc: e.tensor_copy(
                        wcs_bf[0:1, fc * 128:(fc + 1) * 128], prb[0:1, 0:128]),
                        reads=["prb"], writes=["wcs"])
                ppar = gfc % 2
                PK = f"pm{ppar}"
                for dc in range(DC):
                    for tc in range(NTC):
                        p.op(TE, lambda e, dc=dc, tc=tc, wpar=wpar, ppar=ppar, par=par:
                             e.matmul(
                                 pm[ppar][:, tc * PW: tc * PW + TCW],
                                 wcol[wpar][:, dc * 128:(dc + 1) * 128],
                                 xch[par][:, dc * TCH + tc * TCW: dc * TCH + (tc + 1) * TCW],
                                 start=(dc == 0), stop=False),
                             reads=[WK, XK], writes=[PK])
                for tc in range(NTC):
                    p.op(TE, lambda e, fc=fc, tc=tc, ppar=ppar: e.matmul(
                        pm[ppar][:, tc * PW: tc * PW + TCW],
                        wcs_bf[0:1, fc * 128:(fc + 1) * 128],
                        negm_bf[0:1, tc * TCW:(tc + 1) * TCW], start=False, stop=False),
                        reads=["wcs", "negm"], writes=[PK])
                    p.op(TE, lambda e, fc=fc, tc=tc, ppar=ppar: e.matmul(
                        pm[ppar][:, tc * PW: tc * PW + TCW],
                        bq_bf[0:1, fc * 128:(fc + 1) * 128],
                        rinv_bf[0:1, tc * TCW:(tc + 1) * TCW], start=False, stop=True),
                        reads=["bq", "rinv"], writes=[PK])
                QK = f"qo{ppar}"
                for tc in range(NTC):
                    p.op(V, lambda e, tc=tc, ppar=ppar: e.tensor_tensor(
                        qo[ppar][:, tc * TCW:(tc + 1) * TCW],
                        pm[ppar][:, tc * PW: tc * PW + TCW],
                        rbc[:, tc * TCW:(tc + 1) * TCW], ALU.mult),
                        reads=[PK, "rbc"], writes=[QK] if tc == 0 else [QK])
                p.op(SY, lambda e, fc=fc, ch=ch, ppar=ppar: e.dma_start(
                    out=qkv_d[fc * 128:(fc + 1) * 128, ch * TCH:(ch + 1) * TCH],
                    in_=qo[ppar][:, :]),
                    reads=[QK], writes=["qkv_d"], dma=True, stream=f"qw{ppar}")
        if "qkv" in dbg:
            p.op(SY, lambda e: e.dma_start(out=dbg_t["qkv"][:, :], in_=qkv_d[:, :]),
                 reads=["qkv_d"], writes=["dbgq"], dma=True, stream="dbg")
    p.barrier()

    with contextlib.ExitStack() as sb2:
      if "B" in phases:
        SBb = lambda n, s, d: sb2.enter_context(nc.sbuf_tensor(n, s, d))
        PSb = lambda n, s: sb2.enter_context(nc.psum_tensor(n, s, F32))
        QCPW, NQCP, QKTW, NTW = c.QCPW, c.NQCP, c.QCPW // 128, c.NTW
        WIN = 8 if KT > 8 else KT
        al_sb = SBb("al_sb", [128, H * KT], F32)
        tri_sb = SBb("tri_sb", [128, max(NP, 1) * TCW], F32)
        p.op(G, lambda e: e.dma_start(out=al_sb[:, :], in_=alibi_tbl[:, :]),
             writes=["al"], dma=True, stream="init")
        for pp_ in range(max(NP, 1)):
            p.op(G, lambda e, pp_=pp_: e.dma_start(
                out=tri_sb[:, pp_ * TCW:(pp_ + 1) * TCW], in_=tri_neg_i[pp_, :, :]),
                writes=["tri"], dma=True, stream="init")
        p.dma_fence()
        q_sb = [SBb(f"q_sb{i}", [128, T], BF16) for i in range(2)]
        k_sb = [SBb(f"k_sb{i}", [128, T], BF16) for i in range(2)]
        v_tm = [SBb(f"v_tm{i}", [128, KT * 128], BF16) for i in range(2)]
        PT = SBb("PT", [128, 2 * WIN * QCPW], BF16)
        den = [SBb(f"denB{i}", [128, QCPW], F32) for i in range(2)]
        rb2 = [SBb(f"rb2{i}", [128, QCPW], F32) for i in range(2)]
        sh_sb = [SBb(f"sh_sb{i}", [128, QCPW], F32) for i in range(2)]
        att = [SBb(f"att{i}", [128, QCPW], BF16) for i in range(2)]
        ps = [PSb(f"psB{i}", [128, NTW * PW]) for i in range(2)]
        po = [PSb(f"poB{i}", [128, NTW * PW]) for i in range(2)]

        def pt_slot(cn, kt):
            return (cn * WIN + (kt % WIN)) * QCPW

        for hp in range(H // 2):
            heads = (2 * hp, 2 * hp + 1)
            for cn, h in enumerate(heads):
                QH, KH, VH = f"qh{cn}", f"kh{cn}", f"vh{cn}"
                p.op(SY, lambda e, h=h, cn=cn: e.dma_start(
                    out=q_sb[cn][:, :], in_=qkv_d[h * 128:(h + 1) * 128, :]),
                    reads=["qkv_d"], writes=[QH], dma=True, stream=f"ql{cn}")
                p.op(SY, lambda e, h=h, cn=cn: e.dma_start(
                    out=k_sb[cn][:, :],
                    in_=qkv_d[H * 128 + h * 128: H * 128 + (h + 1) * 128, :]),
                    reads=["qkv_d"], writes=[KH], dma=True, stream=f"kl{cn}")
                for kt in range(KT):
                    p.op(SY, lambda e, h=h, cn=cn, kt=kt: e.dma_start(
                        out=v_tm[cn][:, kt * 128:(kt + 1) * 128],
                        in_=qkv_d[2 * H * 128 + h * 128: 2 * H * 128 + (h + 1) * 128,
                                  kt * 128:(kt + 1) * 128],
                        transpose=True),
                        reads=["qkv_d"], writes=[VH], dma=True, stream=f"vl{cn}")

            for qcp in range(c.NQCP):
                kmax = QKTW * (qcp + 1)
                for cn, h in enumerate(heads):
                    p.op(SY, lambda e, h=h, qcp=qcp, cn=cn: e.dma_start(
                        out=sh_sb[cn][:, :],
                        in_=shift_tbl[h, :, qcp * QCPW:(qcp + 1) * QCPW]),
                        writes=[f"sh{cn}"], dma=True, stream=f"shl{cn}")

                def pv_mm(cn, kt):
                    st, sp = (kt == 0), (kt == kmax - 1)
                    VH = f"vh{cn}"
                    for tw in range(NTW):
                        p.op(TE, lambda e, cn=cn, kt=kt, tw=tw, st=st, sp=sp: e.matmul(
                            po[cn][:, tw * PW: tw * PW + TCW],
                            v_tm[cn][:, kt * 128:(kt + 1) * 128],
                            PT[:, pt_slot(cn, kt) + tw * TCW:
                                  pt_slot(cn, kt) + (tw + 1) * TCW],
                            start=st, stop=sp),
                            reads=[VH, f"PT{cn}"], writes=[f"po{cn}"])

                def den_acc(cn, kt):
                    halfw = (NTW // 2) * TCW
                    segs = [(G, 0, halfw), (V, halfw, QCPW - halfw)] if NTW > 1 \
                        else [(G, 0, QCPW)]
                    for eng, o, w in segs:
                        if w == 0:
                            continue
                        if kt == 0:
                            p.op(eng, lambda e, cn=cn, kt=kt, o=o, w=w: e.tensor_copy(
                                den[cn][:, o:o + w],
                                PT[:, pt_slot(cn, kt) + o: pt_slot(cn, kt) + o + w]),
                                reads=[f"PT{cn}"], writes=[f"den{cn}"])
                        else:
                            p.op(eng, lambda e, cn=cn, kt=kt, o=o, w=w: e.tensor_tensor(
                                den[cn][:, o:o + w], den[cn][:, o:o + w],
                                PT[:, pt_slot(cn, kt) + o: pt_slot(cn, kt) + o + w],
                                ALU.add), reads=[f"PT{cn}"], writes=[f"den{cn}"])

                for kt in range(kmax):
                    for cn, h in enumerate(heads):
                        QH, KH = f"qh{cn}", f"kh{cn}"
                        PSK = f"psB{cn}"
                        for tw in range(NTW):
                            p.op(TE, lambda e, h=h, qcp=qcp, kt=kt, tw=tw, cn=cn:
                                 e.matmul(
                                     ps[cn][:, tw * PW: tw * PW + TCW],
                                     k_sb[cn][:, kt * 128:(kt + 1) * 128],
                                     q_sb[cn][:, qcp * QCPW + tw * TCW:
                                               qcp * QCPW + (tw + 1) * TCW],
                                     start=True, stop=True),
                                 reads=[KH, QH], writes=[PSK])
                        if kt >= 2:
                            pv_mm(cn, kt - 2)
                        if TCW == PW:
                            p.op(V, lambda e, cn=cn: e.tensor_tensor(
                                ps[cn][:, 0:NTW * PW], ps[cn][:, 0:NTW * PW],
                                sh_sb[cn][:, :], ALU.add),
                                reads=[f"sh{cn}", PSK], writes=[PSK])
                        else:
                            for tw in range(NTW):
                                p.op(V, lambda e, cn=cn, tw=tw: e.tensor_tensor(
                                    ps[cn][:, tw * PW: tw * PW + TCW],
                                    ps[cn][:, tw * PW: tw * PW + TCW],
                                    sh_sb[cn][:, tw * TCW:(tw + 1) * TCW], ALU.add),
                                    reads=[f"sh{cn}", PSK], writes=[PSK])
                        if kt >= kmax - QKTW:
                            tw = (kt * 128 - qcp * QCPW) // TCW
                            pos = (kt * 128 - qcp * QCPW - tw * TCW) // 128
                            p.op(V, lambda e, cn=cn, tw=tw, pos=pos: e.tensor_tensor(
                                ps[cn][:, tw * PW: tw * PW + TCW],
                                ps[cn][:, tw * PW: tw * PW + TCW],
                                tri_sb[:, pos * TCW:(pos + 1) * TCW], ALU.add),
                                reads=["tri", PSK], writes=[PSK])
                            for twf in range(tw):
                                p.op(V, lambda e, cn=cn, twf=twf: e.tensor_scalar(
                                    ps[cn][:, twf * PW: twf * PW + TCW],
                                    ps[cn][:, twf * PW: twf * PW + TCW],
                                    NEG, None, ALU.add),
                                    reads=[PSK], writes=[PSK])
                        if TCW == PW:
                            p.op(S, lambda e, h=h, kt=kt, cn=cn: e.activation(
                                PT[:, pt_slot(cn, kt): pt_slot(cn, kt) + QCPW],
                                ps[cn][:, 0:NTW * PW], AF.Exp,
                                bias=al_sb[:, h * KT + kt: h * KT + kt + 1],
                                scale=ISQ),
                                reads=[PSK, "al"], writes=[f"PT{cn}"])
                        else:
                            for tw in range(NTW):
                                p.op(S, lambda e, h=h, kt=kt, cn=cn, tw=tw:
                                     e.activation(
                                         PT[:, pt_slot(cn, kt) + tw * TCW:
                                               pt_slot(cn, kt) + (tw + 1) * TCW],
                                         ps[cn][:, tw * PW: tw * PW + TCW], AF.Exp,
                                         bias=al_sb[:, h * KT + kt: h * KT + kt + 1],
                                         scale=ISQ),
                                     reads=[PSK, "al"], writes=[f"PT{cn}"])
                        if kt >= 1:
                            den_acc(cn, kt - 1)
                for cn, h in enumerate(heads):
                    for kt in range(max(kmax - 2, 0), kmax):
                        pv_mm(cn, kt)
                    den_acc(cn, kmax - 1)
                for cn, h in enumerate(heads):
                    for tw in range(NTW):
                        p.op(TE, lambda e, cn=cn, tw=tw: e.matmul(
                            ps[cn][0:1, tw * PW: tw * PW + TCW], ones_c_f[:, :],
                            den[cn][:, tw * TCW:(tw + 1) * TCW],
                            start=True, stop=True),
                            reads=[f"den{cn}", "ones_cf"], writes=[f"psB{cn}"])
                    for tw in range(NTW):
                        p.op(V, lambda e, cn=cn, tw=tw: e.reciprocal(
                            rb2[cn][0:1, tw * TCW:(tw + 1) * TCW],
                            ps[cn][0:1, tw * PW: tw * PW + TCW]),
                            reads=[f"psB{cn}"], writes=[f"rd{cn}"])
                    for tw in range(NTW):
                        p.op(TE, lambda e, cn=cn, tw=tw: e.matmul(
                            ps[cn][:, tw * PW: tw * PW + TCW], ones_r_f[:, :],
                            rb2[cn][0:1, tw * TCW:(tw + 1) * TCW],
                            start=True, stop=True),
                            reads=[f"rd{cn}", "ones_r"], writes=[f"psB{cn}"])
                    for tw in range(NTW):
                        p.op(V, lambda e, cn=cn, tw=tw: e.tensor_copy(
                            rb2[cn][:, tw * TCW:(tw + 1) * TCW],
                            ps[cn][:, tw * PW: tw * PW + TCW]),
                            reads=[f"psB{cn}"], writes=[f"rb2{cn}", f"rd{cn}"])
                    for tw in range(NTW):
                        p.op(V, lambda e, cn=cn, tw=tw: e.tensor_tensor(
                            att[cn][:, tw * TCW:(tw + 1) * TCW],
                            po[cn][:, tw * PW: tw * PW + TCW],
                            rb2[cn][:, tw * TCW:(tw + 1) * TCW], ALU.mult),
                            reads=[f"po{cn}", f"rb2{cn}"], writes=[f"att{cn}"])
                    half = (qcp * QCPW) // OWN
                    off = qcp * QCPW - half * OWN
                    p.op(SY, lambda e, h=h, cn=cn, half=half, off=off: e.dma_start(
                        out=T01_d[half, h * 128:(h + 1) * 128, off:off + QCPW],
                        in_=att[cn][:, :]),
                        reads=[f"att{cn}"], writes=[f"T{half}"], dma=True,
                        stream=f"aw{cn}")
        if "attn" in dbg:
            p.op(SY, lambda e: e.dma_start(out=dbg_t["attn0"][:, :],
                                           in_=T01_d[0, :, :]),
                 reads=["T0"], writes=["dbga0"], dma=True, stream="dbg")
            p.op(SY, lambda e: e.dma_start(out=dbg_t["attn1"][:, :],
                                           in_=T01_d[1, :, :]),
                 reads=["T1"], writes=["dbga1"], dma=True, stream="dbg")
    p.barrier()

    if "X" in phases:
      p.op(G, lambda e: e.collective_compute(
        "AllGather", ALU.bypass, replica_groups=[[0, 1], [2, 3], [4, 5], [6, 7]],
        ins=[T01_d.ap().opt()], outs=[R01_d.ap().opt()]),
        reads=["T0", "T1"], writes=["R0", "R1"], cc=True)
    p.barrier(skip=("cc",))

    with contextlib.ExitStack() as sc:
      if "C" in phases:
        SBc = lambda n, s, d: sc.enter_context(nc.sbuf_tensor(n, s, d))
        PSc = lambda n, s: sc.enter_context(nc.psum_tensor(n, s, F32))
        CS = getattr(c, "CSPLIT", 2 if OWN >= 2048 else 1)
        OWNC = OWN // CS
        W2 = min(c.QCPW, OWNC)
        NT2 = OWNC // W2
        NW2 = W2 // TCW
        NOC = OWNC // TCW

        BB = SBc("BB", [128, DC * OWNC], BF16)
        cbuf = SBc("cbuf", [128, 2 * W2], BF16)
        cnd = [cbuf.ap()[:, 0:W2], cbuf.ap()[:, W2:2 * W2]]
        wco = [SBc(f"wco{i}", [128, DC * 128], BF16) for i in range(2)]
        a1s = [cbuf.ap()[:, 0:W2], cbuf.ap()[:, W2:2 * W2]]
        sq2 = SBc("sq2", [128, 2 * TCW], BF16)
        rows2 = rowsS
        negm2 = negmS
        rbc2 = rbcS
        xa_sb = SBc("xa_sb", [128, 2 * OWNC], BF16)
        b2_bf = bqb2
        Bfw_sb = SBc("Bfw_sb", [128, 2 * D], BF16)
        A_sb = SBc("A_sb", [128, DC * 256], BF16)
        xo1 = [SBc(f"xo1{i}", [128, W2], F32) for i in range(2)]
        a1c = [SBc(f"a1c{i}", [128, W2], BF16) for i in range(2)]
        tmp2 = [SBc(f"tmp2{i}", [128, W2], F32) for i in range(2)]
        fin = [SBc(f"fin{i}", [128, W2], F32) for i in range(2)]
        tb = [SBc(f"tb{i}", [128, 512], F32) for i in range(2)]
        pm2 = [PSc(f"pm2{i}", [128, NW2 * PW]) for i in range(2)]
        ptb = [PSc(f"ptb{i}", [128, 512]) for i in range(2)]

        p.op(G, lambda e: e.dma_start(
            out=b2_bf[0:1, 0:D], in_=bffn.ap().rearrange("(one f) -> one f", one=1)),
            writes=["b2"], dma=True, stream="init")
        p.op(G, lambda e: e.dma_start(
            out=A_sb.ap().rearrange("p (dc er) -> p dc er", er=256),
            in_=A_T3[:, :, :]),
            writes=["A"], dma=True, stream="Ainit")
        p.op(G, lambda e: e.dma_start(out=Epad[:, 0:128], in_=E0_i[:, :]),
             writes=["Epad"], dma=True, stream="epad0")
        p.op(G, lambda e: e.dma_start(out=Epad[:, 128:256], in_=E1_i[:, :]),
             writes=["Epad2"], dma=True, stream="epad1")
        p.op(V, lambda e: e.memset(we_tmp[:, :], 0.0), writes=["we"])
        p.op(G, lambda e: e.dma_start(
            out=we_tmp[0:16, 0:1],
            in_=coeffs.ap().rearrange("(p one) -> p one", one=1)),
            reads=["we"], writes=["we_c"], dma=True, stream="wec")
        p.op(G, lambda e: e.dma_start(
            out=we_tmp[0:16, 1:2],
            in_=scales.ap().rearrange("(p one) -> p one", one=1)),
            reads=["we"], writes=["we_s"], dma=True, stream="wes")
        p.op(V, lambda e: e.tensor_mul(we_tmp[:, 2:3], we_tmp[:, 0:1],
                                       we_tmp[:, 1:2]),
             reads=["we", "we_c", "we_s"], writes=["we_p"])
        p.op(TE, lambda e: e.matmul(pm2[0][:, 0:1], Epad[:, 0:128], we_tmp[:, 2:3],
                                    start=True, stop=True),
             reads=["Epad", "we_p"], writes=["pm20"])
        p.op(TE, lambda e: e.matmul(pm2[0][:, 1:2], Epad[:, 128:256], we_tmp[:, 2:3],
                                    start=True, stop=True),
             reads=["Epad2", "we_p"], writes=["pm20"])
        p.op(V, lambda e: e.tensor_copy(wexp_sb[:, :], pm2[0][:, 0:2]),
             reads=["pm20"], writes=["wexp"])
        NBF = D // W2
        for erc in range(2):
            for bchunk in range(NBF):
                p.op(G, lambda e, erc=erc, bchunk=bchunk: e.dma_start(
                    out=xo1[0][:, :],
                    in_=Bf_er[erc * 128:(erc + 1) * 128,
                              bchunk * W2:(bchunk + 1) * W2]),
                    reads=["bfs_r"], writes=["bfs"], dma=True, stream="bfs")
                p.op(S, lambda e, erc=erc, bchunk=bchunk: e.activation(
                    Bfw_sb[:, erc * D + bchunk * W2: erc * D + (bchunk + 1) * W2],
                    xo1[0][:, :], AF.Copy,
                    bias=0.0, scale=wexp_sb[:, erc:erc + 1]),
                    reads=["bfs", "wexp"], writes=[f"Bfw{erc}", "bfs_r"])
        for erc in range(2):
            for dc in range(DC):
                p.op(TE, lambda e, erc=erc, dc=dc: e.matmul(
                    pm2[0][0:1, 256 + erc * 128:384 + erc * 128],
                    ones_c_bf[:, :],
                    A_sb[:, dc * 256 + erc * 128: dc * 256 + erc * 128 + 128],
                    start=(dc == 0), stop=(dc == DC - 1)),
                    reads=["A", "ones_c"], writes=["pm20"])
        p.op(V, lambda e: e.tensor_copy(acol_bf[0:1, :], pm2[0][0:1, 256:512]),
             reads=["pm20"], writes=["acol"])
        p.dma_fence()

        for chalf in range(CS):
          TB = chalf * OWNC

          for fc in range(DC):
            for hw in range(NT2):
                sl = slice(TB + hw * W2, TB + (hw + 1) * W2)
                bsl = BB[:, fc * OWNC + hw * W2: fc * OWNC + (hw + 1) * W2]
                if fc < DC // 2:
                    srcs = [(T01_d[0], fc, "T0"), (T01_d[1], fc, "T1")]
                    coefs = [sel_sb[:, 0:1], sel_sb[:, 1:2]]
                    selk = ["selT", "selT"]
                else:
                    j = fc - DC // 2
                    srcs = [(R01_d[0, 0], j, "R0"), (R01_d[1, 0], j, "R0"),
                            (R01_d[0, 1], j, "R1"), (R01_d[1, 1], j, "R1")]
                    coefs = [sel_sb[:, 2:3], sel_sb[:, 3:4],
                             sel_sb[:, 4:5], sel_sb[:, 5:6]]
                    selk = ["selR"] * 4
                for i, (src, r, kk) in enumerate(srcs):
                    ci = i % 2
                    p.op(SY, lambda e, src=src, r=r, sl=sl, ci=ci: e.dma_start(
                        out=cnd[ci][:, :], in_=src[r * 128:(r + 1) * 128, sl]),
                        reads=[kk], writes=[f"cnd{ci}"], dma=True, stream=f"cn{ci}")
                    if i == 0:
                        p.op(V, lambda e, ci=ci, bsl=bsl, cf=coefs[i]: e.tensor_scalar(
                            bsl, cnd[ci][:, :], cf, None, ALU.mult),
                            reads=[f"cnd{ci}", selk[i]], writes=["BB"])
                    else:
                        p.op(V, lambda e, ci=ci, bsl=bsl, cf=coefs[i]:
                             e.scalar_tensor_tensor(
                                 bsl, cnd[ci][:, :], cf, bsl, ALU.mult, ALU.add),
                             reads=[f"cnd{ci}", selk[i], "BB"], writes=["BB"])

          for oc in range(DC):
            wpar = oc % 2
            WK = f"wco{wpar}"
            p.op(G, lambda e, oc=oc, wpar=wpar: e.dma_start(
                out=wco[wpar].ap().rearrange("p (dc f) -> p dc f", f=128),
                in_=woutT_blk[oc, :, :, :]),
                writes=[WK], dma=True, stream=f"wc{wpar}")
            for tcp in range(NT2):
                ppar = (oc * NT2 + tcp) % 2
                PK = f"pm2{ppar}"
                for dc in range(DC):
                    for tw in range(NW2):
                        p.op(TE, lambda e, dc=dc, tcp=tcp, tw=tw, wpar=wpar, ppar=ppar:
                             e.matmul(
                                 pm2[ppar][:, tw * PW: tw * PW + TCW],
                                 wco[wpar][:, dc * 128:(dc + 1) * 128],
                                 BB[:, dc * OWNC + tcp * W2 + tw * TCW:
                                       dc * OWNC + tcp * W2 + (tw + 1) * TCW],
                                 start=(dc == 0), stop=(dc == DC - 1)),
                             reads=[WK, "BB"], writes=[PK])
                ab = ppar
                for tw in range(NW2):
                    p.op(S, lambda e, oc=oc, ab=ab, ppar=ppar, tw=tw: e.activation(
                        a1s[ab][:, tw * TCW:(tw + 1) * TCW],
                        pm2[ppar][:, tw * PW: tw * PW + TCW], AF.Identity,
                        bias=bout_sb[:, oc:oc + 1]),
                        reads=[PK, "boutc"], writes=[f"a1s{ab}"])
                p.op(SY, lambda e, oc=oc, tcp=tcp, ab=ab, TB=TB: e.dma_start(
                    out=attn1_d[oc * 128:(oc + 1) * 128,
                                TB + tcp * W2: TB + (tcp + 1) * W2],
                    in_=a1s[ab][:, :]),
                    reads=[f"a1s{ab}"], writes=["attn1"], dma=True, stream=f"a1w{ab}")

          for dc in range(DC):
            for tcp in range(NT2):
                bpar = (dc * NT2 + tcp) % 2
                p.op(SY, lambda e, dc=dc, tcp=tcp, bpar=bpar, TB=TB: e.dma_start(
                    out=xo1[bpar][:, :],
                    in_=x_fm_own[dc * 128:(dc + 1) * 128,
                                 TB + tcp * W2: TB + (tcp + 1) * W2]),
                    writes=[f"xo{bpar}"], dma=True, stream=f"xol{bpar}")
                p.op(SY, lambda e, dc=dc, tcp=tcp, bpar=bpar, TB=TB: e.dma_start(
                    out=a1c[bpar][:, :],
                    in_=attn1_d[dc * 128:(dc + 1) * 128,
                                TB + tcp * W2: TB + (tcp + 1) * W2]),
                    reads=["attn1"], writes=[f"a1c{bpar}"], dma=True,
                    stream=f"a1l{bpar}")
                p.op(V, lambda e, dc=dc, tcp=tcp, bpar=bpar: e.tensor_add(
                    BB[:, dc * OWNC + tcp * W2: dc * OWNC + (tcp + 1) * W2],
                    xo1[bpar][:, :], a1c[bpar][:, :]),
                    reads=[f"xo{bpar}", f"a1c{bpar}"], writes=["BB"])
          if "x1" in dbg:
            for dc in range(DC):
                p.op(SY, lambda e, dc=dc, TB=TB: e.dma_start(
                    out=dbg_t["x1"][dc * 128:(dc + 1) * 128, TB:TB + OWNC],
                    in_=BB[:, dc * OWNC:(dc + 1) * OWNC]),
                    reads=["BB"], writes=["dbgx1"], dma=True, stream="dbg")

          for tc in range(NOC):
            for grp in range(DC // 2):
                for j in range(2):
                    dc = grp * 2 + j
                    p.op(S, lambda e, dc=dc, j=j, tc=tc: e.activation(
                        sq2[:, j * TCW:(j + 1) * TCW],
                        BB[:, dc * OWNC + tc * TCW: dc * OWNC + (tc + 1) * TCW],
                        AF.Square), reads=["BB"], writes=[f"sqc{j}"])
                for j in range(2):
                    dc = grp * 2 + j
                    st, sp = (dc == 0), (dc == DC - 1)
                    p.op(TE, lambda e, dc=dc, tc=tc, st=st, sp=sp: e.matmul(
                        pm2[0][0:1, 0:TCW], ones_c_bf[:, :],
                        BB[:, dc * OWNC + tc * TCW: dc * OWNC + (tc + 1) * TCW],
                        start=st, stop=sp), reads=["BB", "ones_c"], writes=["pm20"])
                    p.op(TE, lambda e, j=j, st=st, sp=sp: e.matmul(
                        pm2[1][0:1, 0:TCW], ones_c_bf[:, :],
                        sq2[:, j * TCW:(j + 1) * TCW],
                        start=st, stop=sp), reads=[f"sqc{j}", "ones_c"],
                        writes=["pm21"])
            R = rows2
            p.op(V, lambda e: e.tensor_scalar(
                R[0:1, 0:TCW], pm2[0][0:1, 0:TCW], 1.0 / D, None, ALU.mult),
                reads=["pm20"], writes=["rows"])
            p.op(V, lambda e: e.tensor_scalar(
                R[0:1, TCW:2 * TCW], pm2[1][0:1, 0:TCW], 1.0 / D, None, ALU.mult),
                reads=["pm21"], writes=["rows"])
            p.op(V, lambda e: e.tensor_mul(
                R[0:1, 2 * TCW:3 * TCW], R[0:1, 0:TCW], R[0:1, 0:TCW]),
                reads=["rows"], writes=["rows"])
            p.op(V, lambda e: e.tensor_sub(
                R[0:1, 2 * TCW:3 * TCW], R[0:1, TCW:2 * TCW], R[0:1, 2 * TCW:3 * TCW]),
                reads=["rows"], writes=["rows"])
            p.op(V, lambda e: e.tensor_scalar(
                R[0:1, 2 * TCW:3 * TCW], R[0:1, 2 * TCW:3 * TCW], EPS, None,
                ALU.add), reads=["rows"], writes=["rows"])
            p.op(S, lambda e: e.activation(
                R[0:1, TCW:2 * TCW], R[0:1, 2 * TCW:3 * TCW], AF.Ln),
                reads=["rows"], writes=["rows"])
            p.op(S, lambda e: e.activation(
                R[0:1, 3 * TCW:4 * TCW], R[0:1, TCW:2 * TCW], AF.Exp, scale=-0.5),
                reads=["rows"], writes=["rows"])
            p.op(V, lambda e, tc=tc: e.tensor_scalar(
                negm2[0:1, tc * TCW:(tc + 1) * TCW], R[0:1, 0:TCW], -1.0, None,
                ALU.mult), reads=["rows"], writes=["negm2"])
            p.op(TE, lambda e, tc=tc: e.matmul(
                pm2[0][:, 0:TCW], ones_r_f[:, :], R[0:1, 3 * TCW:4 * TCW],
                start=True, stop=True), reads=["rows", "ones_r"], writes=["pm20"])
            p.op(V, lambda e, tc=tc: e.tensor_copy(
                rbc2[:, tc * TCW:(tc + 1) * TCW], pm2[0][:, 0:TCW]),
                reads=["pm20"], writes=["rbc2"])

          for erc in range(2):
            for tcp in range(NT2):
                ppar = (erc * NT2 + tcp) % 2
                PK = f"pm2{ppar}"
                for dc in range(DC):
                    for tw in range(NW2):
                        p.op(TE, lambda e, dc=dc, erc=erc, tcp=tcp, tw=tw, ppar=ppar:
                             e.matmul(
                                 pm2[ppar][:, tw * PW: tw * PW + TCW],
                                 A_sb[:, dc * 256 + erc * 128:
                                         dc * 256 + erc * 128 + 128],
                                 BB[:, dc * OWNC + tcp * W2 + tw * TCW:
                                       dc * OWNC + tcp * W2 + (tw + 1) * TCW],
                                 start=(dc == 0), stop=False),
                             reads=["A", "BB"], writes=[PK])
                for tw in range(NW2):
                    p.op(TE, lambda e, erc=erc, tcp=tcp, tw=tw, ppar=ppar: e.matmul(
                        pm2[ppar][:, tw * PW: tw * PW + TCW],
                        acol_bf[0:1, erc * 128:(erc + 1) * 128],
                        negm2[0:1, tcp * W2 + tw * TCW: tcp * W2 + (tw + 1) * TCW],
                        start=False, stop=True),
                        reads=["acol", "negm2"], writes=[PK])
                for tw in range(NW2):
                    p.op(V, lambda e, erc=erc, tcp=tcp, ppar=ppar, tw=tw:
                         e.tensor_tensor(
                             xa_sb[:, erc * OWNC + tcp * W2 + tw * TCW:
                                      erc * OWNC + tcp * W2 + (tw + 1) * TCW],
                             pm2[ppar][:, tw * PW: tw * PW + TCW],
                             rbc2[:, tcp * W2 + tw * TCW: tcp * W2 + (tw + 1) * TCW],
                             ALU.mult),
                         reads=[PK, "rbc2"], writes=["xa"])

          for oc in range(DC):
            wpar = oc % 2
            WK = f"wco{wpar}"
            p.op(G, lambda e, oc=oc, wpar=wpar: e.dma_start(
                out=wco[wpar].ap().rearrange("p (dc f) -> p dc f", f=128),
                in_=WT_blk[oc, :, :, :]),
                writes=[WK], dma=True, stream=f"wc{wpar}")
            for dc in range(DC):
                p.op(TE, lambda e, dc=dc, wpar=wpar: e.matmul(
                    ptb[0][0:1, 0:128], ones_c_bf[:, :],
                    wco[wpar][:, dc * 128:(dc + 1) * 128],
                    start=(dc == 0), stop=(dc == DC - 1)),
                    reads=[WK, "ones_c"], writes=["ptb0"])
            p.op(V, lambda e: e.tensor_copy(wcs2[0:1, :], ptb[0][0:1, 0:128]),
                 reads=["ptb0"], writes=["wcs2"])
            for tcp in range(NT2):
                ppar = tcp % 2
                PK = f"pm2{ppar}"
                for dc in range(DC):
                    for tw in range(NW2):
                        p.op(TE, lambda e, dc=dc, tcp=tcp, tw=tw, wpar=wpar, ppar=ppar:
                             e.matmul(
                                 pm2[ppar][:, tw * PW: tw * PW + TCW],
                                 wco[wpar][:, dc * 128:(dc + 1) * 128],
                                 BB[:, dc * OWNC + tcp * W2 + tw * TCW:
                                       dc * OWNC + tcp * W2 + (tw + 1) * TCW],
                                 start=(dc == 0), stop=False),
                             reads=[WK, "BB"], writes=[PK])
                for tw in range(NW2):
                    p.op(TE, lambda e, tcp=tcp, tw=tw, ppar=ppar: e.matmul(
                        pm2[ppar][:, tw * PW: tw * PW + TCW],
                        wcs2[0:1, :],
                        negm2[0:1, tcp * W2 + tw * TCW: tcp * W2 + (tw + 1) * TCW],
                        start=False, stop=True),
                        reads=["wcs2", "negm2"], writes=[PK])
                for tw in range(NW2):
                    p.op(V, lambda e, tcp=tcp, ppar=ppar, tw=tw: e.tensor_tensor(
                        tmp2[ppar][:, tw * TCW:(tw + 1) * TCW],
                        pm2[ppar][:, tw * PW: tw * PW + TCW],
                        rbc2[:, tcp * W2 + tw * TCW: tcp * W2 + (tw + 1) * TCW],
                        ALU.mult),
                        reads=[PK, "rbc2"], writes=[f"tmp{ppar}"])
                for tw in range(NW2):
                    for erc in range(2):
                        p.op(TE, lambda e, oc=oc, tcp=tcp, tw=tw, erc=erc, ppar=ppar:
                             e.matmul(
                                 pm2[ppar][:, tw * PW: tw * PW + TCW],
                                 Bfw_sb[:, erc * D + oc * 128:
                                           erc * D + (oc + 1) * 128],
                                 xa_sb[:, erc * OWNC + tcp * W2 + tw * TCW:
                                          erc * OWNC + tcp * W2 + (tw + 1) * TCW],
                                 start=(erc == 0), stop=False),
                             reads=[f"Bfw{erc}", "xa", f"tmp{ppar}"], writes=[PK])
                    p.op(TE, lambda e, oc=oc, tw=tw, ppar=ppar: e.matmul(
                        pm2[ppar][:, tw * PW: tw * PW + TCW],
                        b2_bf[0:1, oc * 128:(oc + 1) * 128],
                        ones_row_bf[0:1, :], start=False, stop=True),
                        reads=["b2", "ones_row"], writes=[PK])
                p.op(SY, lambda e, oc=oc, tcp=tcp, ppar=ppar, TB=TB: e.dma_start(
                    out=xo1[ppar][:, :],
                    in_=x_fm_own[oc * 128:(oc + 1) * 128,
                                 TB + tcp * W2: TB + (tcp + 1) * W2]),
                    writes=[f"xo{ppar}"], dma=True, stream=f"xol{ppar}")
                p.op(SY, lambda e, oc=oc, tcp=tcp, ppar=ppar, TB=TB: e.dma_start(
                    out=a1c[ppar][:, :],
                    in_=attn1_d[oc * 128:(oc + 1) * 128,
                                TB + tcp * W2: TB + (tcp + 1) * W2]),
                    reads=["attn1"], writes=[f"a1c{ppar}"], dma=True,
                    stream=f"a1l{ppar}")
                for tw in range(NW2):
                    p.op(V, lambda e, ppar=ppar, tw=tw: e.tensor_tensor(
                        fin[ppar][:, tw * TCW:(tw + 1) * TCW],
                        pm2[ppar][:, tw * PW: tw * PW + TCW],
                        tmp2[ppar][:, tw * TCW:(tw + 1) * TCW], ALU.add),
                        reads=[PK, f"tmp{ppar}"], writes=[f"fin{ppar}"])
                p.op(V, lambda e, ppar=ppar: e.tensor_tensor(
                    fin[ppar][:, :], fin[ppar][:, :], xo1[ppar][:, :], ALU.add),
                    reads=[f"fin{ppar}", f"xo{ppar}"], writes=[f"fin{ppar}"])
                p.op(V, lambda e, ppar=ppar: e.tensor_tensor(
                    fin[ppar][:, :], fin[ppar][:, :], a1c[ppar][:, :], ALU.add),
                    reads=[f"fin{ppar}", f"a1c{ppar}"], writes=[f"fin{ppar}"])
                NTT = W2 // 128
                nb = (NTT + 3) // 4
                for half in range(nb):
                    n_in_b = min(4, NTT - half * 4)
                    tpp = half % 2
                    for i in range(n_in_b):
                        tt = half * 4 + i
                        p.op(TE, lambda e, tt=tt, i=i, ppar=ppar, tpp=tpp: e.transpose(
                            ptb[tpp][:, i * 128:(i + 1) * 128],
                            fin[ppar][:, tt * 128:(tt + 1) * 128], ident[:, :]),
                            reads=[f"fin{ppar}", "ident"], writes=[f"ptb{tpp}"])
                    p.op(V, lambda e, tpp=tpp, n_in_b=n_in_b: e.tensor_copy(
                        tb[tpp][:, 0:n_in_b * 128], ptb[tpp][:, 0:n_in_b * 128]),
                        reads=[f"ptb{tpp}"], writes=[f"tb{tpp}"])
                    p.op(SY, lambda e, oc=oc, tcp=tcp, half=half, tpp=tpp,
                         n_in_b=n_in_b, TB=TB: e.dma_start(
                             out=out_e[TB + tcp * W2 + half * 512:
                                       TB + tcp * W2 + half * 512 + n_in_b * 128,
                                       oc * 128:(oc + 1) * 128].rearrange(
                                           "(n p) f -> p n f", p=128),
                             in_=tb[tpp].ap().rearrange(
                                 "p (n f) -> p n f", f=128)[:, 0:n_in_b, :]),
                         reads=[f"tb{tpp}"], writes=["out"], dma=True,
                         stream=f"ow{tpp}")
    p.barrier()
    p.replay()
    top.close()
    return nc, dbg_t


def wcol_ap(wco, dc):
    return wco[:, dc * 128:(dc + 1) * 128]


from concourse.bass_utils import run_bass_kernel_spmd

_CACHE = {}


def kernel(L0_attn_x, L0_attn_w_qkv, L0_attn_b_qkv, L0_attn_w_out, L0_attn_b_out,
           L0_ffn_W, L0_ffn_bias, L0_ffn_A, L0_ffn_Bf, L0_ffn_coeffs, L0_ffn_scales):
    cfg = CFG()
    in_maps = host_prepare(
        cfg, L0_attn_x, L0_attn_w_qkv, L0_attn_b_qkv, L0_attn_w_out, L0_attn_b_out,
        L0_ffn_W, L0_ffn_bias, L0_ffn_A, L0_ffn_Bf, L0_ffn_coeffs, L0_ffn_scales)
    if "nc" not in _CACHE:
        _CACHE["nc"] = build_nc(cfg)[0]
    nc = _CACHE["nc"]
    res = run_bass_kernel_spmd(nc, in_maps, core_ids=list(range(8)))
    out = np.empty((4, cfg.T, cfg.D), np.float32)
    for c in range(8):
        b, me = c // 2, c % 2
        out[b, cfg.OWN * me: cfg.OWN * (me + 1)] = res.results[c]["out_e"]
    return out


# version 11 (final)
# speedup vs baseline: 1.2527x; 1.2527x over previous
import math
import contextlib
from dataclasses import dataclass

import numpy as np
import ml_dtypes

import concourse.bass as bass
import concourse.mybir as mybir

F32 = mybir.dt.float32
BF16 = mybir.dt.bfloat16
AF = mybir.ActivationFunctionType
ALU = mybir.AluOpType
BFD = ml_dtypes.bfloat16

NEG = -1.0e12
PW = 512
EPS = 1e-5
ISQ = 1.0 / math.sqrt(128.0)
SQ = math.sqrt(128.0)


@dataclass
class CFG:
    T: int = 4096
    D: int = 4096
    H: int = 16
    TCH: int = 1024
    TCW: int = 512
    QCPW: int = 1024

    @property
    def HT(self): return 2 * self.H
    @property
    def F(self): return 3 * self.H * 128
    @property
    def FC(self): return self.F // 128
    @property
    def DC(self): return self.D // 128
    @property
    def KT(self): return self.T // 128
    @property
    def OWN(self): return self.T // 2
    @property
    def NCH(self): return self.T // self.TCH
    @property
    def NTC(self): return self.TCH // self.TCW
    @property
    def NQCP(self): return self.T // self.QCPW
    @property
    def NTW(self): return self.QCPW // self.TCW
    @property
    def NOC(self): return self.OWN // self.TCW


def alibi_slopes(n_heads, alibi_bias_max=8):
    _n = 2 ** math.ceil(math.log2(n_heads))
    m = np.arange(1, _n + 1, dtype=np.float64) * (alibi_bias_max / _n)
    slopes = 1.0 / np.power(2.0, m)
    if _n != n_heads:
        slopes = np.concatenate([slopes[1::2], slopes[::2]])[:n_heads]
    return slopes.astype(np.float32)



def host_prepare(cfg, x, w_qkv, b_qkv, w_out, b_out, W, bffn, A, Bf, coeffs, scales):
    c = cfg
    T, D, H = c.T, c.D, c.H
    slopes = alibi_slopes(c.HT)

    def blocked(mT):
        Dd, N = mT.shape
        return np.ascontiguousarray(
            mT.reshape(Dd // 128, 128, N // 128, 128).transpose(2, 1, 0, 3))

    W = np.asarray(W); w_qkv = np.asarray(w_qkv); w_out = np.asarray(w_out)
    A = np.asarray(A); Bf = np.asarray(Bf)
    WT_blk = blocked(np.ascontiguousarray(W.T))
    A_flatT = np.ascontiguousarray(A.reshape(256, D).T)
    A_T3 = np.ascontiguousarray(A_flatT.reshape(c.DC, 128, 256).transpose(1, 0, 2))
    Bf_er = np.ascontiguousarray(Bf.transpose(0, 2, 1).reshape(256, D))

    ident = np.eye(128, dtype=np.float32)
    NP = c.TCW // 128
    tri = np.zeros((max(NP, 1), 128, c.TCW), dtype=np.float32)
    for pp in range(NP):
        for k in range(128):
            tri[pp, k, : pp * 128 + k] = NEG
    E0 = np.zeros((128, 128), dtype=np.float32)
    E1 = np.zeros((128, 128), dtype=np.float32)
    for pp in range(128):
        E0[pp // 16, pp] = 1.0
        E1[8 + pp // 16, pp] = 1.0

    ins = []
    for core in range(8):
        b, me = core // 2, core % 2
        xb = np.ascontiguousarray(np.asarray(x)[b].T.astype(np.float32))
        rq = slice(H * 128 * me, H * 128 * (me + 1))
        rk = slice(D + H * 128 * me, D + H * 128 * (me + 1))
        rv = slice(2 * D + H * 128 * me, 2 * D + H * 128 * (me + 1))
        wq = np.concatenate([w_qkv[rq], w_qkv[rk], w_qkv[rv]], axis=0)
        wqkvT_blk = blocked(np.ascontiguousarray(wq.T))
        bq = np.concatenate([np.asarray(b_qkv)[rq], np.asarray(b_qkv)[rk],
                             np.asarray(b_qkv)[rv]]).astype(np.float32)
        woT = np.ascontiguousarray(w_out.T)
        order = np.concatenate([
            np.arange(H * 128 * me, H * 128 * (me + 1)),
            np.arange(H * 128 * (1 - me), H * 128 * (2 - me))])
        woutT_blk = blocked(np.ascontiguousarray(woT[order]))
        hs = slopes[H * me: H * me + H]
        al = np.zeros((128, H * c.KT), dtype=np.float32)
        for h in range(H):
            for kt in range(c.KT):
                al[:, h * c.KT + kt] = hs[h] * ((kt * 128 + np.arange(128)) - (T - 1))
        sh = np.zeros((H, 128, T), dtype=np.float32)
        qq = np.arange(T, dtype=np.float64)
        for h in range(H):
            sh[h, :, :] = (hs[h] * (T - 1 - qq) * SQ).astype(np.float32)[None, :]
        selT = np.zeros((128, 2), dtype=np.float32)
        selT[:, me] = 1.0
        selR = np.zeros((128, 4), dtype=np.float32)
        selR[:, 2 * me + (1 - me)] = 1.0
        d = dict(
            x_fm=xb,
            x_fm_own=np.ascontiguousarray(xb[:, c.OWN * me: c.OWN * (me + 1)]),
            wqkvT_blk=wqkvT_blk, bqkv=bq,
            woutT_blk=woutT_blk, bout=np.asarray(b_out).astype(np.float32),
            WT_blk=WT_blk, bffn=np.asarray(bffn).astype(np.float32),
            A_T3=A_T3, Bf_er=Bf_er,
            coeffs=np.asarray(coeffs).astype(np.float32),
            scales=np.asarray(scales).astype(np.float32),
            alibi_tbl=al, shift_tbl=sh, tri_neg=tri,
            E0=E0, E1=E1, ident_in=ident, selT=selT, selR=selR,
        )
        ins.append({k: np.ascontiguousarray(v) for k, v in d.items()})
    return ins



ENGN = ("tensor", "vector", "scalar", "sync", "gpsimd")
DMA_ENG = {"sync", "gpsimd"}


class Prog:
    def __init__(self, nc):
        self.nc = nc
        self.ops = {e: [] for e in ENGN}
        self.waited = {e: {} for e in ENGN}
        self.sems = {}
        for e in ENGN:
            self.sems[e] = [nc.alloc_semaphore(name=f"se_{e}"), 0]
        self.sems["cc"] = [nc.alloc_semaphore(name="se_cc"), 0]
        self.buf = {}

    def _sem_handle(self, name):
        return self.sems[name][0]

    def stream_sem(self, name):
        name = "ds_" + name
        if name not in self.sems:
            self.sems[name] = [self.nc.alloc_semaphore(name=name), 0]
        return name

    def op(self, eng, fn, reads=(), writes=(), extra=(), dma=False, cc=False,
           stream=None):
        waits = {}
        def add(sn, v):
            if v and waits.get(sn, 0) < v:
                waits[sn] = v
        for k in reads:
            b = self.buf.get(k)
            if b:
                for sn, v in b["w"].items():
                    add(sn, v)
        for k in writes:
            b = self.buf.get(k)
            if b:
                for sn, v in b["w"].items():
                    add(sn, v)
                for sn, v in b["r"].items():
                    add(sn, v)
        for sn, v in extra:
            add(sn, v)
        wl = []
        for sn, v in waits.items():
            if sn == eng == "tensor":
                continue
            if self.waited[eng].get(sn, -1) >= v:
                continue
            self.waited[eng][sn] = v
            wl.append((self._sem_handle(sn), v))
        if cc:
            sname = "cc"
        elif dma:
            assert stream is not None, "DMA ops need a stream name"
            sname = self.stream_sem(stream)
        else:
            sname = eng
        sem, val = self.sems[sname]
        k_inc = 16 if dma else 1
        val += k_inc
        self.sems[sname][1] = val
        mile = (sname, val)

        def run(e, wl=wl, fn=fn, sem=sem, k_inc=k_inc):
            for h, v in wl:
                e.wait_ge(h, v)
            ins = fn(e)
            ins.then_inc(sem, k_inc)
        self.ops[eng].append(run)
        for k in writes:
            b = self.buf.setdefault(k, {"w": {}, "r": {}})
            if mile[0] in b["w"] or (not dma and not cc):
                b["w"] = {mile[0]: mile[1]}
                b["r"] = {}
            else:
                b["w"][mile[0]] = mile[1]
                b["r"] = {}
        for k in reads:
            b = self.buf.setdefault(k, {"w": {}, "r": {}})
            if b["r"].get(mile[0], 0) < mile[1]:
                b["r"][mile[0]] = mile[1]
        return mile

    def wait_only(self, eng, extra):
        wl = []
        for sn, v in extra:
            if self.waited[eng].get(sn, -1) >= v:
                continue
            self.waited[eng][sn] = v
            wl.append((self._sem_handle(sn), v))
        if wl:
            def run(e, wl=wl):
                for h, v in wl:
                    e.wait_ge(h, v)
            self.ops[eng].append(run)

    def dma_fence(self):
        vals = {sn: v[1] for sn, v in self.sems.items() if sn not in ENGN}
        for e in ENGN:
            self.wait_only(e, [(sn, v) for sn, v in vals.items() if v])

    def barrier(self, skip=()):
        vals = {sn: self.sems[sn][1] for sn in self.sems if sn not in skip}
        for e in ENGN:
            self.wait_only(e, [(sn, v) for sn, v in vals.items() if sn != e])
        marks = {}
        for e in ENGN:
            marks[e] = self.op(e, self.mark_fns[e],
                               reads=["barz"], writes=[f"bar_{e}"])
        for e in ENGN:
            self.wait_only(e, [(o, marks[o][1]) for o in ENGN if o != e])

    def replay(self):
        nc = self.nc
        with nc.Block() as block:
            @block.tensor
            def _(t):
                for f in self.ops["tensor"]:
                    f(t)

            @block.vector
            def _(v):
                for f in self.ops["vector"]:
                    f(v)

            @block.scalar
            def _(s):
                for f in self.ops["scalar"]:
                    f(s)

            @block.sync
            def _(sy):
                for f in self.ops["sync"]:
                    f(sy)

            @block.gpsimd
            def _(g):
                for f in self.ops["gpsimd"]:
                    f(g)


def _nop(e):
    try:
        return e.engine_nop()
    except AttributeError:
        return e.nop()


def build_nc(cfg, dbg=(), phases=("A", "B", "X", "C")):
    c = cfg
    nc = bass.Bass("TRN2", target_bir_lowering=False, num_devices=8)
    T, D, H, F = c.T, c.D, c.H, c.F
    DC, FC, KT, OWN, TCW = c.DC, c.FC, c.KT, c.OWN, c.TCW
    NP = TCW // 128

    di = lambda n, s: nc.dram_tensor(n, s, F32, kind="ExternalInput")
    x_fm = di("x_fm", [D, T])
    x_fm_own = di("x_fm_own", [D, OWN])
    wqkvT_blk = di("wqkvT_blk", [FC, 128, DC, 128])
    bqkv = di("bqkv", [F])
    woutT_blk = di("woutT_blk", [DC, 128, DC, 128])
    bout = di("bout", [D])
    WT_blk = di("WT_blk", [DC, 128, DC, 128])
    bffn = di("bffn", [D])
    A_T3 = di("A_T3", [128, DC, 256])
    Bf_er = di("Bf_er", [256, D])
    coeffs = di("coeffs", [16])
    scales = di("scales", [16])
    alibi_tbl = di("alibi_tbl", [128, H * KT])
    shift_tbl = di("shift_tbl", [H, 128, T])
    tri_neg_i = di("tri_neg", [max(NP, 1), 128, TCW])
    E0_i = di("E0", [128, 128])
    E1_i = di("E1", [128, 128])
    ident_i = di("ident_in", [128, 128])
    selT_i = di("selT", [128, 2])
    selR_i = di("selR", [128, 4])

    out_e = nc.dram_tensor("out_e", [OWN, D], F32, kind="ExternalOutput")
    qkv_d = nc.dram_tensor("qkv_d", [F, T], BF16, kind="Internal")
    T01_d = nc.dram_tensor("T01_d", [2, H * 128, OWN], BF16, kind="Internal")
    R01_d = nc.dram_tensor("R01_d", [2, 2, H * 128, OWN], BF16, kind="Internal")
    attn1_d = nc.dram_tensor("attn1_d", [D, OWN], BF16, kind="Internal")

    dbg_t = {}
    if "qkv" in dbg:
        dbg_t["qkv"] = nc.dram_tensor("dbg_qkv", [F, T], BF16, kind="ExternalOutput")
    if "attn" in dbg:
        dbg_t["attn0"] = nc.dram_tensor("dbg_attn0", [H * 128, OWN], BF16,
                                        kind="ExternalOutput")
        dbg_t["attn1"] = nc.dram_tensor("dbg_attn1", [H * 128, OWN], BF16,
                                        kind="ExternalOutput")
    if "x1" in dbg:
        dbg_t["x1"] = nc.dram_tensor("dbg_x1", [D, OWN], BF16, kind="ExternalOutput")

    p = Prog(nc)
    top = contextlib.ExitStack()
    SB = lambda n, s, d: top.enter_context(nc.sbuf_tensor(n, s, d))
    bar_sb = SB("bar_sb", [1, 8], F32)
    bar_bf = SB("bar_bf", [1, 2], BF16)
    p.mark_fns = {
        "tensor": lambda e: e.ldweights(bar_bf[0:1, 0:1]),
        "vector": lambda e: e.memset(bar_sb[0:1, 1:2], 0.0),
        "scalar": lambda e: e.copy(bar_sb[0:1, 2:3], bar_sb[0:1, 0:1]),
        "sync": _nop,
        "gpsimd": _nop,
    }

    ones_c_bf = SB("ones_c_bf", [128, 1], BF16)
    ones_c_f = SB("ones_c_f", [128, 1], F32)
    ones_r_f = SB("ones_r_f", [1, 128], F32)
    ones_row_bf = SB("ones_row_bf", [1, TCW], BF16)
    ident = SB("ident", [128, 128], F32)
    sel_sb = SB("sel_sb", [128, 6], F32)
    bout_sb = SB("bout_sb", [128, DC], F32)
    rowsS = SB("rowsS", [1, 4 * TCW], F32)
    bqb2 = SB("bqb2", [1, F], BF16)
    wcs_bf = SB("wcs_bf", [1, F], BF16)
    negmS = SB("negmS", [1, max(c.TCH, OWN)], BF16)
    rinv_bf = SB("rinv_bf", [1, c.TCH], BF16)
    rbcS = SB("rbcS", [128, max(c.TCH, OWN)], F32)
    acol_bf = SB("acol_bf", [1, 256], BF16)
    wexp_sb = SB("wexp_sb", [128, 2], F32)
    wcs2 = SB("wcs2", [1, 128], BF16)
    we_tmp = SB("we_tmp", [128, 4], F32)
    Epad = SB("Epad", [128, 256], F32)

    G, SY, V, S, TE = "gpsimd", "sync", "vector", "scalar", "tensor"

    p.op(G, lambda e: e.dma_start(out=ident[:, :], in_=ident_i[:, :]),
         writes=["ident"], dma=True, stream="init")
    p.op(G, lambda e: e.dma_start(out=sel_sb[:, 0:2], in_=selT_i[:, :]),
         writes=["selT"], dma=True, stream="init")
    p.op(G, lambda e: e.dma_start(out=sel_sb[:, 2:6], in_=selR_i[:, :]),
         writes=["selR"], dma=True, stream="init")
    for oc in range(DC):
        p.op(G, lambda e, oc=oc: e.dma_start(
            out=bout_sb[:, oc:oc + 1],
            in_=bout.ap()[oc * 128:(oc + 1) * 128].rearrange("(p one) -> p one", one=1)),
            writes=["boutc"], dma=True, stream="init")

    p.op(V, lambda e: e.memset(bar_sb[0:1, :], 0.0), writes=["barz"])
    p.op(V, lambda e: e.memset(bar_bf[0:1, :], 1.0), writes=["barz"])
    p.op(V, lambda e: e.memset(ones_c_bf[:, :], 1.0), writes=["ones_c"])
    p.op(V, lambda e: e.memset(ones_c_f[:, :], 1.0), writes=["ones_cf"])
    p.op(V, lambda e: e.memset(ones_r_f[:, :], 1.0), writes=["ones_r"])
    p.op(V, lambda e: e.memset(ones_row_bf[:, :], 1.0), writes=["ones_row"])

    p.barrier()

    with contextlib.ExitStack() as sa:
      if "A" in phases:
        SBa = lambda n, s, d: sa.enter_context(nc.sbuf_tensor(n, s, d))
        PSa = lambda n, s: sa.enter_context(nc.psum_tensor(n, s, F32))
        TCH, NTC, NCH = c.TCH, c.NTC, c.NCH
        xch = [SBa(f"xch{i}", [128, DC * TCH], BF16) for i in range(2)]
        wcol = [SBa(f"wcol{i}", [128, DC * 128], BF16) for i in range(2)]
        qo = [SBa(f"qo{i}", [128, TCH], BF16) for i in range(2)]
        sq4 = SBa("sq4", [128, 2 * TCH], BF16)
        rows = rowsS
        negm_bf = negmS
        rbc = rbcS
        bq_bf = bqb2
        pm = [PSa(f"pmA{i}", [128, NTC * PW]) for i in range(2)]
        prb = PSa("prbA", [128, NTC * PW])
        p.op(G, lambda e: e.dma_start(
            out=bq_bf[0:1, :], in_=bqkv.ap().rearrange("(one f) -> one f", one=1)),
            writes=["bq"], dma=True, stream="init")
        p.dma_fence()

        for ch in range(NCH):
            par = ch % 2
            XK = f"xch{par}"
            for dc in range(DC):
                p.op(G, lambda e, dc=dc, ch=ch, par=par: e.dma_start(
                    out=xch[par][:, dc * TCH:(dc + 1) * TCH],
                    in_=x_fm[dc * 128:(dc + 1) * 128, ch * TCH:(ch + 1) * TCH]),
                    writes=[XK], dma=True, stream=f"xc{par}")

            for grp in range(DC // 2):
                for j in range(2):
                    dc = grp * 2 + j
                    p.op(S, lambda e, dc=dc, j=j, par=par: e.activation(
                        sq4[:, j * TCH:(j + 1) * TCH],
                        xch[par][:, dc * TCH:(dc + 1) * TCH], AF.Square),
                        reads=[XK], writes=[f"sq{j}"])
                for j in range(2):
                    dc = grp * 2 + j
                    st, sp = (dc == 0), (dc == DC - 1)
                    for tc in range(NTC):
                        p.op(TE, lambda e, dc=dc, tc=tc, par=par, st=st, sp=sp: e.matmul(
                            pm[0][0:1, tc * PW: tc * PW + TCW], ones_c_bf[:, :],
                            xch[par][:, dc * TCH + tc * TCW: dc * TCH + (tc + 1) * TCW],
                            start=st, stop=sp),
                            reads=[XK, "ones_c"], writes=["pm0"])
                        p.op(TE, lambda e, j=j, tc=tc, st=st, sp=sp: e.matmul(
                            pm[1][0:1, tc * PW: tc * PW + TCW], ones_c_bf[:, :],
                            sq4[:, j * TCH + tc * TCW: j * TCH + (tc + 1) * TCW],
                            start=st, stop=sp),
                            reads=[f"sq{j}", "ones_c"], writes=["pm1"])
            R = rows
            for tc in range(NTC):
                o = tc * TCW
                p.op(V, lambda e, tc=tc: e.tensor_scalar(
                    R[0:1, 0:TCW], pm[0][0:1, tc * PW: tc * PW + TCW],
                    1.0 / D, None, ALU.mult), reads=["pm0"], writes=["rows"])
                p.op(V, lambda e, tc=tc: e.tensor_scalar(
                    R[0:1, TCW:2 * TCW], pm[1][0:1, tc * PW: tc * PW + TCW],
                    1.0 / D, None, ALU.mult), reads=["pm1"], writes=["rows"])
                p.op(V, lambda e: e.tensor_mul(
                    R[0:1, 2 * TCW:3 * TCW], R[0:1, 0:TCW], R[0:1, 0:TCW]),
                    reads=["rows"], writes=["rows"])
                p.op(V, lambda e: e.tensor_sub(
                    R[0:1, 2 * TCW:3 * TCW], R[0:1, TCW:2 * TCW], R[0:1, 2 * TCW:3 * TCW]),
                    reads=["rows"], writes=["rows"])
                p.op(V, lambda e: e.tensor_scalar(
                    R[0:1, 2 * TCW:3 * TCW], R[0:1, 2 * TCW:3 * TCW], EPS, None,
                    ALU.add), reads=["rows"], writes=["rows"])
                p.op(S, lambda e: e.activation(
                    R[0:1, TCW:2 * TCW], R[0:1, 2 * TCW:3 * TCW], AF.Ln),
                    reads=["rows"], writes=["rows"])
                p.op(S, lambda e: e.activation(
                    R[0:1, 3 * TCW:4 * TCW], R[0:1, TCW:2 * TCW], AF.Exp, scale=-0.5),
                    reads=["rows"], writes=["rows"])
                p.op(S, lambda e, o=o: e.activation(
                    rinv_bf[0:1, o:o + TCW], R[0:1, TCW:2 * TCW], AF.Exp, scale=0.5),
                    reads=["rows"], writes=["rinv"])
                p.op(V, lambda e, o=o: e.tensor_scalar(
                    negm_bf[0:1, o:o + TCW], R[0:1, 0:TCW], -1.0, None, ALU.mult),
                    reads=["rows"], writes=["negm"])
                p.op(TE, lambda e, tc=tc: e.matmul(
                    prb[:, tc * PW: tc * PW + TCW], ones_r_f[:, :],
                    R[0:1, 3 * TCW:4 * TCW], start=True, stop=True),
                    reads=["rows", "ones_r"], writes=["prb"])
                p.op(V, lambda e, o=o, tc=tc: e.tensor_copy(
                    rbc[:, o:o + TCW], prb[:, tc * PW: tc * PW + TCW]),
                    reads=["prb"], writes=["rbc"])

            for fc in range(FC):
                gfc = ch * FC + fc
                wpar = gfc % 2
                WK = f"wcol{wpar}"
                p.op(G, lambda e, fc=fc, wpar=wpar: e.dma_start(
                    out=wcol[wpar].ap().rearrange("p (dc f) -> p dc f", f=128),
                    in_=wqkvT_blk[fc, :, :, :]),
                    writes=[WK], dma=True, stream=f"wl{wpar}")
                if ch == 0:
                    for dc in range(DC):
                        p.op(TE, lambda e, dc=dc, wpar=wpar: e.matmul(
                            prb[0:1, 0:128], ones_c_bf[:, :],
                            wcol[wpar][:, dc * 128:(dc + 1) * 128],
                            start=(dc == 0), stop=(dc == DC - 1)),
                            reads=[WK, "ones_c"], writes=["prb"])
                    p.op(V, lambda e, fc=fc: e.tensor_copy(
                        wcs_bf[0:1, fc * 128:(fc + 1) * 128], prb[0:1, 0:128]),
                        reads=["prb"], writes=["wcs"])
                ppar = gfc % 2
                PK = f"pm{ppar}"
                for dc in range(DC):
                    for tc in range(NTC):
                        p.op(TE, lambda e, dc=dc, tc=tc, wpar=wpar, ppar=ppar, par=par:
                             e.matmul(
                                 pm[ppar][:, tc * PW: tc * PW + TCW],
                                 wcol[wpar][:, dc * 128:(dc + 1) * 128],
                                 xch[par][:, dc * TCH + tc * TCW: dc * TCH + (tc + 1) * TCW],
                                 start=(dc == 0), stop=False),
                             reads=[WK, XK], writes=[PK])
                for tc in range(NTC):
                    p.op(TE, lambda e, fc=fc, tc=tc, ppar=ppar: e.matmul(
                        pm[ppar][:, tc * PW: tc * PW + TCW],
                        wcs_bf[0:1, fc * 128:(fc + 1) * 128],
                        negm_bf[0:1, tc * TCW:(tc + 1) * TCW], start=False, stop=False),
                        reads=["wcs", "negm"], writes=[PK])
                    p.op(TE, lambda e, fc=fc, tc=tc, ppar=ppar: e.matmul(
                        pm[ppar][:, tc * PW: tc * PW + TCW],
                        bq_bf[0:1, fc * 128:(fc + 1) * 128],
                        rinv_bf[0:1, tc * TCW:(tc + 1) * TCW], start=False, stop=True),
                        reads=["bq", "rinv"], writes=[PK])
                QK = f"qo{ppar}"
                for tc in range(NTC):
                    p.op(V, lambda e, tc=tc, ppar=ppar: e.tensor_tensor(
                        qo[ppar][:, tc * TCW:(tc + 1) * TCW],
                        pm[ppar][:, tc * PW: tc * PW + TCW],
                        rbc[:, tc * TCW:(tc + 1) * TCW], ALU.mult),
                        reads=[PK, "rbc"], writes=[QK] if tc == 0 else [QK])
                p.op(SY, lambda e, fc=fc, ch=ch, ppar=ppar: e.dma_start(
                    out=qkv_d[fc * 128:(fc + 1) * 128, ch * TCH:(ch + 1) * TCH],
                    in_=qo[ppar][:, :]),
                    reads=[QK], writes=["qkv_d"], dma=True, stream=f"qw{ppar}")
        if "qkv" in dbg:
            p.op(SY, lambda e: e.dma_start(out=dbg_t["qkv"][:, :], in_=qkv_d[:, :]),
                 reads=["qkv_d"], writes=["dbgq"], dma=True, stream="dbg")
    p.barrier()

    with contextlib.ExitStack() as sb2:
      if "B" in phases:
        SBb = lambda n, s, d: sb2.enter_context(nc.sbuf_tensor(n, s, d))
        PSb = lambda n, s: sb2.enter_context(nc.psum_tensor(n, s, F32))
        QCPW, NQCP, QKTW, NTW = c.QCPW, c.NQCP, c.QCPW // 128, c.NTW
        WIN = 8 if KT > 8 else KT
        al_sb = SBb("al_sb", [128, H * KT], F32)
        tri_sb = SBb("tri_sb", [128, max(NP, 1) * TCW], F32)
        p.op(G, lambda e: e.dma_start(out=al_sb[:, :], in_=alibi_tbl[:, :]),
             writes=["al"], dma=True, stream="init")
        for pp_ in range(max(NP, 1)):
            p.op(G, lambda e, pp_=pp_: e.dma_start(
                out=tri_sb[:, pp_ * TCW:(pp_ + 1) * TCW], in_=tri_neg_i[pp_, :, :]),
                writes=["tri"], dma=True, stream="init")
        p.dma_fence()
        q_sb = [SBb(f"q_sb{i}", [128, T], BF16) for i in range(2)]
        k_sb = [SBb(f"k_sb{i}", [128, T], BF16) for i in range(2)]
        v_tm = [SBb(f"v_tm{i}", [128, KT * 128], BF16) for i in range(2)]
        PT = SBb("PT", [128, 2 * WIN * QCPW], BF16)
        den = [SBb(f"denB{i}", [128, QCPW], F32) for i in range(2)]
        rb2 = [SBb(f"rb2{i}", [128, QCPW], F32) for i in range(2)]
        sh_sb = [SBb(f"sh_sb{i}", [128, QCPW], F32) for i in range(2)]
        att = [SBb(f"att{i}", [128, QCPW], BF16) for i in range(2)]
        ps = [PSb(f"psB{i}", [128, NTW * PW]) for i in range(2)]
        po = [PSb(f"poB{i}", [128, NTW * PW]) for i in range(2)]

        def pt_slot(cn, kt):
            return (cn * WIN + (kt % WIN)) * QCPW

        for hp in range(H // 2):
            heads = (2 * hp, 2 * hp + 1)
            for cn, h in enumerate(heads):
                QH, KH, VH = f"qh{cn}", f"kh{cn}", f"vh{cn}"
                p.op(SY, lambda e, h=h, cn=cn: e.dma_start(
                    out=q_sb[cn][:, :], in_=qkv_d[h * 128:(h + 1) * 128, :]),
                    reads=["qkv_d"], writes=[QH], dma=True, stream=f"ql{cn}")
                p.op(SY, lambda e, h=h, cn=cn: e.dma_start(
                    out=k_sb[cn][:, :],
                    in_=qkv_d[H * 128 + h * 128: H * 128 + (h + 1) * 128, :]),
                    reads=["qkv_d"], writes=[KH], dma=True, stream=f"kl{cn}")
                for kt in range(KT):
                    p.op(SY, lambda e, h=h, cn=cn, kt=kt: e.dma_start(
                        out=v_tm[cn][:, kt * 128:(kt + 1) * 128],
                        in_=qkv_d[2 * H * 128 + h * 128: 2 * H * 128 + (h + 1) * 128,
                                  kt * 128:(kt + 1) * 128],
                        transpose=True),
                        reads=["qkv_d"], writes=[VH], dma=True, stream=f"vl{cn}")

            for qcp in range(c.NQCP):
                kmax = QKTW * (qcp + 1)
                for cn, h in enumerate(heads):
                    p.op(SY, lambda e, h=h, qcp=qcp, cn=cn: e.dma_start(
                        out=sh_sb[cn][:, :],
                        in_=shift_tbl[h, :, qcp * QCPW:(qcp + 1) * QCPW]),
                        writes=[f"sh{cn}"], dma=True, stream=f"shl{cn}")

                def pv_mm(cn, kt):
                    st, sp = (kt == 0), (kt == kmax - 1)
                    VH = f"vh{cn}"
                    for tw in range(NTW):
                        p.op(TE, lambda e, cn=cn, kt=kt, tw=tw, st=st, sp=sp: e.matmul(
                            po[cn][:, tw * PW: tw * PW + TCW],
                            v_tm[cn][:, kt * 128:(kt + 1) * 128],
                            PT[:, pt_slot(cn, kt) + tw * TCW:
                                  pt_slot(cn, kt) + (tw + 1) * TCW],
                            start=st, stop=sp),
                            reads=[VH, f"PT{cn}"], writes=[f"po{cn}"])

                def den_acc(cn, kt):
                    if kt == 0:
                        p.op(G, lambda e, cn=cn, kt=kt: e.tensor_copy(
                            den[cn][:, :],
                            PT[:, pt_slot(cn, kt): pt_slot(cn, kt) + QCPW]),
                            reads=[f"PT{cn}"], writes=[f"den{cn}"])
                    else:
                        p.op(G, lambda e, cn=cn, kt=kt: e.tensor_tensor(
                            den[cn][:, :], den[cn][:, :],
                            PT[:, pt_slot(cn, kt): pt_slot(cn, kt) + QCPW],
                            ALU.add), reads=[f"PT{cn}"], writes=[f"den{cn}"])

                for kt in range(kmax):
                    for cn, h in enumerate(heads):
                        QH, KH = f"qh{cn}", f"kh{cn}"
                        PSK = f"psB{cn}"
                        for tw in range(NTW):
                            p.op(TE, lambda e, h=h, qcp=qcp, kt=kt, tw=tw, cn=cn:
                                 e.matmul(
                                     ps[cn][:, tw * PW: tw * PW + TCW],
                                     k_sb[cn][:, kt * 128:(kt + 1) * 128],
                                     q_sb[cn][:, qcp * QCPW + tw * TCW:
                                               qcp * QCPW + (tw + 1) * TCW],
                                     start=True, stop=True),
                                 reads=[KH, QH], writes=[PSK])
                        if kt >= 2:
                            pv_mm(cn, kt - 2)
                        if TCW == PW:
                            p.op(V, lambda e, cn=cn: e.tensor_tensor(
                                ps[cn][:, 0:NTW * PW], ps[cn][:, 0:NTW * PW],
                                sh_sb[cn][:, :], ALU.add),
                                reads=[f"sh{cn}", PSK], writes=[PSK])
                        else:
                            for tw in range(NTW):
                                p.op(V, lambda e, cn=cn, tw=tw: e.tensor_tensor(
                                    ps[cn][:, tw * PW: tw * PW + TCW],
                                    ps[cn][:, tw * PW: tw * PW + TCW],
                                    sh_sb[cn][:, tw * TCW:(tw + 1) * TCW], ALU.add),
                                    reads=[f"sh{cn}", PSK], writes=[PSK])
                        if kt >= kmax - QKTW:
                            tw = (kt * 128 - qcp * QCPW) // TCW
                            pos = (kt * 128 - qcp * QCPW - tw * TCW) // 128
                            p.op(V, lambda e, cn=cn, tw=tw, pos=pos: e.tensor_tensor(
                                ps[cn][:, tw * PW: tw * PW + TCW],
                                ps[cn][:, tw * PW: tw * PW + TCW],
                                tri_sb[:, pos * TCW:(pos + 1) * TCW], ALU.add),
                                reads=["tri", PSK], writes=[PSK])
                            for twf in range(tw):
                                p.op(V, lambda e, cn=cn, twf=twf: e.tensor_scalar(
                                    ps[cn][:, twf * PW: twf * PW + TCW],
                                    ps[cn][:, twf * PW: twf * PW + TCW],
                                    NEG, None, ALU.add),
                                    reads=[PSK], writes=[PSK])
                        if TCW == PW:
                            p.op(S, lambda e, h=h, kt=kt, cn=cn: e.activation(
                                PT[:, pt_slot(cn, kt): pt_slot(cn, kt) + QCPW],
                                ps[cn][:, 0:NTW * PW], AF.Exp,
                                bias=al_sb[:, h * KT + kt: h * KT + kt + 1],
                                scale=ISQ),
                                reads=[PSK, "al"], writes=[f"PT{cn}"])
                        else:
                            for tw in range(NTW):
                                p.op(S, lambda e, h=h, kt=kt, cn=cn, tw=tw:
                                     e.activation(
                                         PT[:, pt_slot(cn, kt) + tw * TCW:
                                               pt_slot(cn, kt) + (tw + 1) * TCW],
                                         ps[cn][:, tw * PW: tw * PW + TCW], AF.Exp,
                                         bias=al_sb[:, h * KT + kt: h * KT + kt + 1],
                                         scale=ISQ),
                                     reads=[PSK, "al"], writes=[f"PT{cn}"])
                        if kt >= 1:
                            den_acc(cn, kt - 1)
                for cn, h in enumerate(heads):
                    for kt in range(max(kmax - 2, 0), kmax):
                        pv_mm(cn, kt)
                    den_acc(cn, kmax - 1)
                for cn, h in enumerate(heads):
                    for tw in range(NTW):
                        p.op(TE, lambda e, cn=cn, tw=tw: e.matmul(
                            ps[cn][0:1, tw * PW: tw * PW + TCW], ones_c_f[:, :],
                            den[cn][:, tw * TCW:(tw + 1) * TCW],
                            start=True, stop=True),
                            reads=[f"den{cn}", "ones_cf"], writes=[f"psB{cn}"])
                    for tw in range(NTW):
                        p.op(V, lambda e, cn=cn, tw=tw: e.reciprocal(
                            rb2[cn][0:1, tw * TCW:(tw + 1) * TCW],
                            ps[cn][0:1, tw * PW: tw * PW + TCW]),
                            reads=[f"psB{cn}"], writes=[f"rd{cn}"])
                    for tw in range(NTW):
                        p.op(TE, lambda e, cn=cn, tw=tw: e.matmul(
                            ps[cn][:, tw * PW: tw * PW + TCW], ones_r_f[:, :],
                            rb2[cn][0:1, tw * TCW:(tw + 1) * TCW],
                            start=True, stop=True),
                            reads=[f"rd{cn}", "ones_r"], writes=[f"psB{cn}"])
                    for tw in range(NTW):
                        p.op(V, lambda e, cn=cn, tw=tw: e.tensor_copy(
                            rb2[cn][:, tw * TCW:(tw + 1) * TCW],
                            ps[cn][:, tw * PW: tw * PW + TCW]),
                            reads=[f"psB{cn}"], writes=[f"rb2{cn}", f"rd{cn}"])
                    for tw in range(NTW):
                        p.op(V, lambda e, cn=cn, tw=tw: e.tensor_tensor(
                            att[cn][:, tw * TCW:(tw + 1) * TCW],
                            po[cn][:, tw * PW: tw * PW + TCW],
                            rb2[cn][:, tw * TCW:(tw + 1) * TCW], ALU.mult),
                            reads=[f"po{cn}", f"rb2{cn}"], writes=[f"att{cn}"])
                    half = (qcp * QCPW) // OWN
                    off = qcp * QCPW - half * OWN
                    p.op(SY, lambda e, h=h, cn=cn, half=half, off=off: e.dma_start(
                        out=T01_d[half, h * 128:(h + 1) * 128, off:off + QCPW],
                        in_=att[cn][:, :]),
                        reads=[f"att{cn}"], writes=[f"T{half}"], dma=True,
                        stream=f"aw{cn}")
        if "attn" in dbg:
            p.op(SY, lambda e: e.dma_start(out=dbg_t["attn0"][:, :],
                                           in_=T01_d[0, :, :]),
                 reads=["T0"], writes=["dbga0"], dma=True, stream="dbg")
            p.op(SY, lambda e: e.dma_start(out=dbg_t["attn1"][:, :],
                                           in_=T01_d[1, :, :]),
                 reads=["T1"], writes=["dbga1"], dma=True, stream="dbg")
    p.barrier()

    if "X" in phases:
      p.op(G, lambda e: e.collective_compute(
        "AllGather", ALU.bypass, replica_groups=[[0, 1], [2, 3], [4, 5], [6, 7]],
        ins=[T01_d.ap().opt()], outs=[R01_d.ap().opt()]),
        reads=["T0", "T1"], writes=["R0", "R1"], cc=True)
    p.barrier(skip=("cc",))

    with contextlib.ExitStack() as sc:
      if "C" in phases:
        SBc = lambda n, s, d: sc.enter_context(nc.sbuf_tensor(n, s, d))
        PSc = lambda n, s: sc.enter_context(nc.psum_tensor(n, s, F32))
        CS = getattr(c, "CSPLIT", 2 if OWN >= 2048 else 1)
        OWNC = OWN // CS
        W2 = min(c.QCPW, OWNC)
        NT2 = OWNC // W2
        NW2 = W2 // TCW
        NOC = OWNC // TCW

        BB = SBc("BB", [128, DC * OWNC], BF16)
        cbuf = SBc("cbuf", [128, 2 * W2], BF16)
        cnd = [cbuf.ap()[:, 0:W2], cbuf.ap()[:, W2:2 * W2]]
        wco = [SBc(f"wco{i}", [128, DC * 128], BF16) for i in range(2)]
        a1s = [cbuf.ap()[:, 0:W2], cbuf.ap()[:, W2:2 * W2]]
        sq2 = SBc("sq2", [128, 2 * TCW], BF16)
        rows2 = rowsS
        negm2 = negmS
        rbc2 = rbcS
        xa_sb = SBc("xa_sb", [128, 2 * OWNC], BF16)
        b2_bf = bqb2
        Bfw_sb = SBc("Bfw_sb", [128, 2 * D], BF16)
        A_sb = SBc("A_sb", [128, DC * 256], BF16)
        xo1 = [SBc(f"xo1{i}", [128, W2], F32) for i in range(2)]
        a1c = [SBc(f"a1c{i}", [128, W2], BF16) for i in range(2)]
        tmp2 = [SBc(f"tmp2{i}", [128, W2], F32) for i in range(2)]
        fin = [SBc(f"fin{i}", [128, W2], F32) for i in range(2)]
        tb = [SBc(f"tb{i}", [128, 512], F32) for i in range(2)]
        pm2 = [PSc(f"pm2{i}", [128, NW2 * PW]) for i in range(2)]
        ptb = [PSc(f"ptb{i}", [128, 512]) for i in range(2)]

        p.op(G, lambda e: e.dma_start(
            out=b2_bf[0:1, 0:D], in_=bffn.ap().rearrange("(one f) -> one f", one=1)),
            writes=["b2"], dma=True, stream="init")
        p.op(G, lambda e: e.dma_start(
            out=A_sb.ap().rearrange("p (dc er) -> p dc er", er=256),
            in_=A_T3[:, :, :]),
            writes=["A"], dma=True, stream="Ainit")
        p.op(G, lambda e: e.dma_start(out=Epad[:, 0:128], in_=E0_i[:, :]),
             writes=["Epad"], dma=True, stream="epad0")
        p.op(G, lambda e: e.dma_start(out=Epad[:, 128:256], in_=E1_i[:, :]),
             writes=["Epad2"], dma=True, stream="epad1")
        p.op(V, lambda e: e.memset(we_tmp[:, :], 0.0), writes=["we"])
        p.op(G, lambda e: e.dma_start(
            out=we_tmp[0:16, 0:1],
            in_=coeffs.ap().rearrange("(p one) -> p one", one=1)),
            reads=["we"], writes=["we_c"], dma=True, stream="wec")
        p.op(G, lambda e: e.dma_start(
            out=we_tmp[0:16, 1:2],
            in_=scales.ap().rearrange("(p one) -> p one", one=1)),
            reads=["we"], writes=["we_s"], dma=True, stream="wes")
        p.op(V, lambda e: e.tensor_mul(we_tmp[:, 2:3], we_tmp[:, 0:1],
                                       we_tmp[:, 1:2]),
             reads=["we", "we_c", "we_s"], writes=["we_p"])
        p.op(TE, lambda e: e.matmul(pm2[0][:, 0:1], Epad[:, 0:128], we_tmp[:, 2:3],
                                    start=True, stop=True),
             reads=["Epad", "we_p"], writes=["pm20"])
        p.op(TE, lambda e: e.matmul(pm2[0][:, 1:2], Epad[:, 128:256], we_tmp[:, 2:3],
                                    start=True, stop=True),
             reads=["Epad2", "we_p"], writes=["pm20"])
        p.op(V, lambda e: e.tensor_copy(wexp_sb[:, :], pm2[0][:, 0:2]),
             reads=["pm20"], writes=["wexp"])
        NBF = D // W2
        for erc in range(2):
            for bchunk in range(NBF):
                p.op(G, lambda e, erc=erc, bchunk=bchunk: e.dma_start(
                    out=xo1[0][:, :],
                    in_=Bf_er[erc * 128:(erc + 1) * 128,
                              bchunk * W2:(bchunk + 1) * W2]),
                    reads=["bfs_r"], writes=["bfs"], dma=True, stream="bfs")
                p.op(S, lambda e, erc=erc, bchunk=bchunk: e.activation(
                    Bfw_sb[:, erc * D + bchunk * W2: erc * D + (bchunk + 1) * W2],
                    xo1[0][:, :], AF.Copy,
                    bias=0.0, scale=wexp_sb[:, erc:erc + 1]),
                    reads=["bfs", "wexp"], writes=[f"Bfw{erc}", "bfs_r"])
        for erc in range(2):
            for dc in range(DC):
                p.op(TE, lambda e, erc=erc, dc=dc: e.matmul(
                    pm2[0][0:1, 256 + erc * 128:384 + erc * 128],
                    ones_c_bf[:, :],
                    A_sb[:, dc * 256 + erc * 128: dc * 256 + erc * 128 + 128],
                    start=(dc == 0), stop=(dc == DC - 1)),
                    reads=["A", "ones_c"], writes=["pm20"])
        p.op(V, lambda e: e.tensor_copy(acol_bf[0:1, :], pm2[0][0:1, 256:512]),
             reads=["pm20"], writes=["acol"])
        p.dma_fence()

        for chalf in range(CS):
          TB = chalf * OWNC

          for fc in range(DC):
            for hw in range(NT2):
                sl = slice(TB + hw * W2, TB + (hw + 1) * W2)
                bsl = BB[:, fc * OWNC + hw * W2: fc * OWNC + (hw + 1) * W2]
                if fc < DC // 2:
                    srcs = [(T01_d[0], fc, "T0"), (T01_d[1], fc, "T1")]
                    coefs = [sel_sb[:, 0:1], sel_sb[:, 1:2]]
                    selk = ["selT", "selT"]
                else:
                    j = fc - DC // 2
                    srcs = [(R01_d[0, 0], j, "R0"), (R01_d[1, 0], j, "R0"),
                            (R01_d[0, 1], j, "R1"), (R01_d[1, 1], j, "R1")]
                    coefs = [sel_sb[:, 2:3], sel_sb[:, 3:4],
                             sel_sb[:, 4:5], sel_sb[:, 5:6]]
                    selk = ["selR"] * 4
                BK = "BBo" if fc < DC // 2 else "BBp"
                for i, (src, r, kk) in enumerate(srcs):
                    ci = i % 2
                    p.op(SY, lambda e, src=src, r=r, sl=sl, ci=ci: e.dma_start(
                        out=cnd[ci][:, :], in_=src[r * 128:(r + 1) * 128, sl]),
                        reads=[kk], writes=[f"cnd{ci}"], dma=True, stream=f"cn{ci}")
                    if i == 0:
                        p.op(V, lambda e, ci=ci, bsl=bsl, cf=coefs[i]: e.tensor_scalar(
                            bsl, cnd[ci][:, :], cf, None, ALU.mult),
                            reads=[f"cnd{ci}", selk[i]], writes=[BK])
                    else:
                        p.op(V, lambda e, ci=ci, bsl=bsl, cf=coefs[i]:
                             e.scalar_tensor_tensor(
                                 bsl, cnd[ci][:, :], cf, bsl, ALU.mult, ALU.add),
                             reads=[f"cnd{ci}", selk[i], BK], writes=[BK])

          for oc in range(DC):
            wpar = oc % 2
            WK = f"wco{wpar}"
            p.op(G, lambda e, oc=oc, wpar=wpar: e.dma_start(
                out=wco[wpar].ap().rearrange("p (dc f) -> p dc f", f=128),
                in_=woutT_blk[oc, :, :, :]),
                writes=[WK], dma=True, stream=f"wc{wpar}")
            for tcp in range(NT2):
                ppar = (oc * NT2 + tcp) % 2
                PK = f"pm2{ppar}"
                for dc in range(DC):
                    for tw in range(NW2):
                        p.op(TE, lambda e, dc=dc, tcp=tcp, tw=tw, wpar=wpar, ppar=ppar:
                             e.matmul(
                                 pm2[ppar][:, tw * PW: tw * PW + TCW],
                                 wco[wpar][:, dc * 128:(dc + 1) * 128],
                                 BB[:, dc * OWNC + tcp * W2 + tw * TCW:
                                       dc * OWNC + tcp * W2 + (tw + 1) * TCW],
                                 start=(dc == 0), stop=(dc == DC - 1)),
                             reads=[WK, "BBo" if dc < DC // 2 else "BBp"],
                             writes=[PK])
                ab = ppar
                for tw in range(NW2):
                    p.op(S, lambda e, oc=oc, ab=ab, ppar=ppar, tw=tw: e.activation(
                        a1s[ab][:, tw * TCW:(tw + 1) * TCW],
                        pm2[ppar][:, tw * PW: tw * PW + TCW], AF.Identity,
                        bias=bout_sb[:, oc:oc + 1]),
                        reads=[PK, "boutc"], writes=[f"a1s{ab}"])
                p.op(SY, lambda e, oc=oc, tcp=tcp, ab=ab, TB=TB: e.dma_start(
                    out=attn1_d[oc * 128:(oc + 1) * 128,
                                TB + tcp * W2: TB + (tcp + 1) * W2],
                    in_=a1s[ab][:, :]),
                    reads=[f"a1s{ab}"], writes=["attn1"], dma=True, stream=f"a1w{ab}")

          for dc in range(DC):
            for tcp in range(NT2):
                bpar = (dc * NT2 + tcp) % 2
                p.op(SY, lambda e, dc=dc, tcp=tcp, bpar=bpar, TB=TB: e.dma_start(
                    out=xo1[bpar][:, :],
                    in_=x_fm_own[dc * 128:(dc + 1) * 128,
                                 TB + tcp * W2: TB + (tcp + 1) * W2]),
                    writes=[f"xo{bpar}"], dma=True, stream=f"xol{bpar}")
                p.op(SY, lambda e, dc=dc, tcp=tcp, bpar=bpar, TB=TB: e.dma_start(
                    out=a1c[bpar][:, :],
                    in_=attn1_d[dc * 128:(dc + 1) * 128,
                                TB + tcp * W2: TB + (tcp + 1) * W2]),
                    reads=["attn1"], writes=[f"a1c{bpar}"], dma=True,
                    stream=f"a1l{bpar}")
                p.op(V, lambda e, dc=dc, tcp=tcp, bpar=bpar: e.tensor_add(
                    BB[:, dc * OWNC + tcp * W2: dc * OWNC + (tcp + 1) * W2],
                    xo1[bpar][:, :], a1c[bpar][:, :]),
                    reads=[f"xo{bpar}", f"a1c{bpar}"],
                    writes=["BBo" if dc < DC // 2 else "BBp"])
          if "x1" in dbg:
            for dc in range(DC):
                p.op(SY, lambda e, dc=dc, TB=TB: e.dma_start(
                    out=dbg_t["x1"][dc * 128:(dc + 1) * 128, TB:TB + OWNC],
                    in_=BB[:, dc * OWNC:(dc + 1) * OWNC]),
                    reads=["BBo", "BBp"], writes=["dbgx1"], dma=True, stream="dbg")

          for tc in range(NOC):
            for grp in range(DC // 2):
                for j in range(2):
                    dc = grp * 2 + j
                    p.op(S, lambda e, dc=dc, j=j, tc=tc: e.activation(
                        sq2[:, j * TCW:(j + 1) * TCW],
                        BB[:, dc * OWNC + tc * TCW: dc * OWNC + (tc + 1) * TCW],
                        AF.Square), reads=["BBo", "BBp"], writes=[f"sqc{j}"])
                for j in range(2):
                    dc = grp * 2 + j
                    st, sp = (dc == 0), (dc == DC - 1)
                    p.op(TE, lambda e, dc=dc, tc=tc, st=st, sp=sp: e.matmul(
                        pm2[0][0:1, 0:TCW], ones_c_bf[:, :],
                        BB[:, dc * OWNC + tc * TCW: dc * OWNC + (tc + 1) * TCW],
                        start=st, stop=sp), reads=["BBo", "BBp", "ones_c"], writes=["pm20"])
                    p.op(TE, lambda e, j=j, st=st, sp=sp: e.matmul(
                        pm2[1][0:1, 0:TCW], ones_c_bf[:, :],
                        sq2[:, j * TCW:(j + 1) * TCW],
                        start=st, stop=sp), reads=[f"sqc{j}", "ones_c"],
                        writes=["pm21"])
            R = rows2
            p.op(V, lambda e: e.tensor_scalar(
                R[0:1, 0:TCW], pm2[0][0:1, 0:TCW], 1.0 / D, None, ALU.mult),
                reads=["pm20"], writes=["rows"])
            p.op(V, lambda e: e.tensor_scalar(
                R[0:1, TCW:2 * TCW], pm2[1][0:1, 0:TCW], 1.0 / D, None, ALU.mult),
                reads=["pm21"], writes=["rows"])
            p.op(V, lambda e: e.tensor_mul(
                R[0:1, 2 * TCW:3 * TCW], R[0:1, 0:TCW], R[0:1, 0:TCW]),
                reads=["rows"], writes=["rows"])
            p.op(V, lambda e: e.tensor_sub(
                R[0:1, 2 * TCW:3 * TCW], R[0:1, TCW:2 * TCW], R[0:1, 2 * TCW:3 * TCW]),
                reads=["rows"], writes=["rows"])
            p.op(V, lambda e: e.tensor_scalar(
                R[0:1, 2 * TCW:3 * TCW], R[0:1, 2 * TCW:3 * TCW], EPS, None,
                ALU.add), reads=["rows"], writes=["rows"])
            p.op(S, lambda e: e.activation(
                R[0:1, TCW:2 * TCW], R[0:1, 2 * TCW:3 * TCW], AF.Ln),
                reads=["rows"], writes=["rows"])
            p.op(S, lambda e: e.activation(
                R[0:1, 3 * TCW:4 * TCW], R[0:1, TCW:2 * TCW], AF.Exp, scale=-0.5),
                reads=["rows"], writes=["rows"])
            p.op(V, lambda e, tc=tc: e.tensor_scalar(
                negm2[0:1, tc * TCW:(tc + 1) * TCW], R[0:1, 0:TCW], -1.0, None,
                ALU.mult), reads=["rows"], writes=["negm2"])
            p.op(TE, lambda e, tc=tc: e.matmul(
                pm2[0][:, 0:TCW], ones_r_f[:, :], R[0:1, 3 * TCW:4 * TCW],
                start=True, stop=True), reads=["rows", "ones_r"], writes=["pm20"])
            p.op(V, lambda e, tc=tc: e.tensor_copy(
                rbc2[:, tc * TCW:(tc + 1) * TCW], pm2[0][:, 0:TCW]),
                reads=["pm20"], writes=["rbc2"])

          for erc in range(2):
            for tcp in range(NT2):
                ppar = (erc * NT2 + tcp) % 2
                PK = f"pm2{ppar}"
                for dc in range(DC):
                    for tw in range(NW2):
                        p.op(TE, lambda e, dc=dc, erc=erc, tcp=tcp, tw=tw, ppar=ppar:
                             e.matmul(
                                 pm2[ppar][:, tw * PW: tw * PW + TCW],
                                 A_sb[:, dc * 256 + erc * 128:
                                         dc * 256 + erc * 128 + 128],
                                 BB[:, dc * OWNC + tcp * W2 + tw * TCW:
                                       dc * OWNC + tcp * W2 + (tw + 1) * TCW],
                                 start=(dc == 0), stop=False),
                             reads=["A", "BBo", "BBp"], writes=[PK])
                for tw in range(NW2):
                    p.op(TE, lambda e, erc=erc, tcp=tcp, tw=tw, ppar=ppar: e.matmul(
                        pm2[ppar][:, tw * PW: tw * PW + TCW],
                        acol_bf[0:1, erc * 128:(erc + 1) * 128],
                        negm2[0:1, tcp * W2 + tw * TCW: tcp * W2 + (tw + 1) * TCW],
                        start=False, stop=True),
                        reads=["acol", "negm2"], writes=[PK])
                for tw in range(NW2):
                    p.op(V, lambda e, erc=erc, tcp=tcp, ppar=ppar, tw=tw:
                         e.tensor_tensor(
                             xa_sb[:, erc * OWNC + tcp * W2 + tw * TCW:
                                      erc * OWNC + tcp * W2 + (tw + 1) * TCW],
                             pm2[ppar][:, tw * PW: tw * PW + TCW],
                             rbc2[:, tcp * W2 + tw * TCW: tcp * W2 + (tw + 1) * TCW],
                             ALU.mult),
                         reads=[PK, "rbc2"], writes=["xa"])

          for oc in range(DC):
            wpar = oc % 2
            WK = f"wco{wpar}"
            p.op(G, lambda e, oc=oc, wpar=wpar: e.dma_start(
                out=wco[wpar].ap().rearrange("p (dc f) -> p dc f", f=128),
                in_=WT_blk[oc, :, :, :]),
                writes=[WK], dma=True, stream=f"wc{wpar}")
            for dc in range(DC):
                p.op(TE, lambda e, dc=dc, wpar=wpar: e.matmul(
                    ptb[0][0:1, 0:128], ones_c_bf[:, :],
                    wco[wpar][:, dc * 128:(dc + 1) * 128],
                    start=(dc == 0), stop=(dc == DC - 1)),
                    reads=[WK, "ones_c"], writes=["ptb0"])
            p.op(V, lambda e: e.tensor_copy(wcs2[0:1, :], ptb[0][0:1, 0:128]),
                 reads=["ptb0"], writes=["wcs2"])
            for tcp in range(NT2):
                ppar = tcp % 2
                PK = f"pm2{ppar}"
                for dc in range(DC):
                    for tw in range(NW2):
                        p.op(TE, lambda e, dc=dc, tcp=tcp, tw=tw, wpar=wpar, ppar=ppar:
                             e.matmul(
                                 pm2[ppar][:, tw * PW: tw * PW + TCW],
                                 wco[wpar][:, dc * 128:(dc + 1) * 128],
                                 BB[:, dc * OWNC + tcp * W2 + tw * TCW:
                                       dc * OWNC + tcp * W2 + (tw + 1) * TCW],
                                 start=(dc == 0), stop=False),
                             reads=[WK, "BBo", "BBp"], writes=[PK])
                for tw in range(NW2):
                    p.op(TE, lambda e, tcp=tcp, tw=tw, ppar=ppar: e.matmul(
                        pm2[ppar][:, tw * PW: tw * PW + TCW],
                        wcs2[0:1, :],
                        negm2[0:1, tcp * W2 + tw * TCW: tcp * W2 + (tw + 1) * TCW],
                        start=False, stop=True),
                        reads=["wcs2", "negm2"], writes=[PK])
                for tw in range(NW2):
                    p.op(V, lambda e, tcp=tcp, ppar=ppar, tw=tw: e.tensor_tensor(
                        tmp2[ppar][:, tw * TCW:(tw + 1) * TCW],
                        pm2[ppar][:, tw * PW: tw * PW + TCW],
                        rbc2[:, tcp * W2 + tw * TCW: tcp * W2 + (tw + 1) * TCW],
                        ALU.mult),
                        reads=[PK, "rbc2"], writes=[f"tmp{ppar}"])
                for tw in range(NW2):
                    for erc in range(2):
                        p.op(TE, lambda e, oc=oc, tcp=tcp, tw=tw, erc=erc, ppar=ppar:
                             e.matmul(
                                 pm2[ppar][:, tw * PW: tw * PW + TCW],
                                 Bfw_sb[:, erc * D + oc * 128:
                                           erc * D + (oc + 1) * 128],
                                 xa_sb[:, erc * OWNC + tcp * W2 + tw * TCW:
                                          erc * OWNC + tcp * W2 + (tw + 1) * TCW],
                                 start=(erc == 0), stop=False),
                             reads=[f"Bfw{erc}", "xa", f"tmp{ppar}"], writes=[PK])
                    p.op(TE, lambda e, oc=oc, tw=tw, ppar=ppar: e.matmul(
                        pm2[ppar][:, tw * PW: tw * PW + TCW],
                        b2_bf[0:1, oc * 128:(oc + 1) * 128],
                        ones_row_bf[0:1, :], start=False, stop=True),
                        reads=["b2", "ones_row"], writes=[PK])
                p.op(SY, lambda e, oc=oc, tcp=tcp, ppar=ppar, TB=TB: e.dma_start(
                    out=xo1[ppar][:, :],
                    in_=x_fm_own[oc * 128:(oc + 1) * 128,
                                 TB + tcp * W2: TB + (tcp + 1) * W2]),
                    writes=[f"xo{ppar}"], dma=True, stream=f"xol{ppar}")
                p.op(SY, lambda e, oc=oc, tcp=tcp, ppar=ppar, TB=TB: e.dma_start(
                    out=a1c[ppar][:, :],
                    in_=attn1_d[oc * 128:(oc + 1) * 128,
                                TB + tcp * W2: TB + (tcp + 1) * W2]),
                    reads=["attn1"], writes=[f"a1c{ppar}"], dma=True,
                    stream=f"a1l{ppar}")
                for tw in range(NW2):
                    p.op(V, lambda e, ppar=ppar, tw=tw: e.tensor_tensor(
                        fin[ppar][:, tw * TCW:(tw + 1) * TCW],
                        pm2[ppar][:, tw * PW: tw * PW + TCW],
                        tmp2[ppar][:, tw * TCW:(tw + 1) * TCW], ALU.add),
                        reads=[PK, f"tmp{ppar}"], writes=[f"fin{ppar}"])
                p.op(V, lambda e, ppar=ppar: e.tensor_tensor(
                    fin[ppar][:, :], fin[ppar][:, :], xo1[ppar][:, :], ALU.add),
                    reads=[f"fin{ppar}", f"xo{ppar}"], writes=[f"fin{ppar}"])
                p.op(V, lambda e, ppar=ppar: e.tensor_tensor(
                    fin[ppar][:, :], fin[ppar][:, :], a1c[ppar][:, :], ALU.add),
                    reads=[f"fin{ppar}", f"a1c{ppar}"], writes=[f"fin{ppar}"])
                NTT = W2 // 128
                nb = (NTT + 3) // 4
                for half in range(nb):
                    n_in_b = min(4, NTT - half * 4)
                    tpp = half % 2
                    for i in range(n_in_b):
                        tt = half * 4 + i
                        p.op(TE, lambda e, tt=tt, i=i, ppar=ppar, tpp=tpp: e.transpose(
                            ptb[tpp][:, i * 128:(i + 1) * 128],
                            fin[ppar][:, tt * 128:(tt + 1) * 128], ident[:, :]),
                            reads=[f"fin{ppar}", "ident"], writes=[f"ptb{tpp}"])
                    p.op(V, lambda e, tpp=tpp, n_in_b=n_in_b: e.tensor_copy(
                        tb[tpp][:, 0:n_in_b * 128], ptb[tpp][:, 0:n_in_b * 128]),
                        reads=[f"ptb{tpp}"], writes=[f"tb{tpp}"])
                    p.op(SY, lambda e, oc=oc, tcp=tcp, half=half, tpp=tpp,
                         n_in_b=n_in_b, TB=TB: e.dma_start(
                             out=out_e[TB + tcp * W2 + half * 512:
                                       TB + tcp * W2 + half * 512 + n_in_b * 128,
                                       oc * 128:(oc + 1) * 128].rearrange(
                                           "(n p) f -> p n f", p=128),
                             in_=tb[tpp].ap().rearrange(
                                 "p (n f) -> p n f", f=128)[:, 0:n_in_b, :]),
                         reads=[f"tb{tpp}"], writes=["out"], dma=True,
                         stream=f"ow{tpp}")
    p.barrier()
    p.replay()
    top.close()
    return nc, dbg_t


def wcol_ap(wco, dc):
    return wco[:, dc * 128:(dc + 1) * 128]


from concourse.bass_utils import run_bass_kernel_spmd

_CACHE = {}


def kernel(L0_attn_x, L0_attn_w_qkv, L0_attn_b_qkv, L0_attn_w_out, L0_attn_b_out,
           L0_ffn_W, L0_ffn_bias, L0_ffn_A, L0_ffn_Bf, L0_ffn_coeffs, L0_ffn_scales):
    cfg = CFG()
    in_maps = host_prepare(
        cfg, L0_attn_x, L0_attn_w_qkv, L0_attn_b_qkv, L0_attn_w_out, L0_attn_b_out,
        L0_ffn_W, L0_ffn_bias, L0_ffn_A, L0_ffn_Bf, L0_ffn_coeffs, L0_ffn_scales)
    if "nc" not in _CACHE:
        _CACHE["nc"] = build_nc(cfg)[0]
    nc = _CACHE["nc"]
    res = run_bass_kernel_spmd(nc, in_maps, core_ids=list(range(8)))
    out = np.empty((4, cfg.T, cfg.D), np.float32)
    for c in range(8):
        b, me = c // 2, c % 2
        out[b, cfg.OWN * me: cfg.OWN * (me + 1)] = res.results[c]["out_e"]
    return out


# version 12
# speedup vs baseline: 1.2818x; 1.0233x over previous
import math
import contextlib
from dataclasses import dataclass

import numpy as np
import ml_dtypes

import concourse.bass as bass
import concourse.mybir as mybir

F32 = mybir.dt.float32
BF16 = mybir.dt.bfloat16
AF = mybir.ActivationFunctionType
ALU = mybir.AluOpType
BFD = ml_dtypes.bfloat16

NEG = -1.0e12
PW = 512
EPS = 1e-5
ISQ = 1.0 / math.sqrt(128.0)
SQ = math.sqrt(128.0)


@dataclass
class CFG:
    T: int = 4096
    D: int = 4096
    H: int = 16
    TCH: int = 1024
    TCW: int = 512
    QCPW: int = 1024

    @property
    def HT(self): return 2 * self.H
    @property
    def F(self): return 3 * self.H * 128
    @property
    def FC(self): return self.F // 128
    @property
    def DC(self): return self.D // 128
    @property
    def KT(self): return self.T // 128
    @property
    def OWN(self): return self.T // 2
    @property
    def NCH(self): return self.T // self.TCH
    @property
    def NTC(self): return self.TCH // self.TCW
    @property
    def NQCP(self): return self.T // self.QCPW
    @property
    def NTW(self): return self.QCPW // self.TCW
    @property
    def NOC(self): return self.OWN // self.TCW


def alibi_slopes(n_heads, alibi_bias_max=8):
    _n = 2 ** math.ceil(math.log2(n_heads))
    m = np.arange(1, _n + 1, dtype=np.float64) * (alibi_bias_max / _n)
    slopes = 1.0 / np.power(2.0, m)
    if _n != n_heads:
        slopes = np.concatenate([slopes[1::2], slopes[::2]])[:n_heads]
    return slopes.astype(np.float32)



def host_prepare(cfg, x, w_qkv, b_qkv, w_out, b_out, W, bffn, A, Bf, coeffs, scales):
    c = cfg
    T, D, H = c.T, c.D, c.H
    slopes = alibi_slopes(c.HT)

    def blocked(mT):
        Dd, N = mT.shape
        return np.ascontiguousarray(
            mT.reshape(Dd // 128, 128, N // 128, 128).transpose(2, 1, 0, 3))

    W = np.asarray(W); w_qkv = np.asarray(w_qkv); w_out = np.asarray(w_out)
    A = np.asarray(A); Bf = np.asarray(Bf)
    WT_blk = blocked(np.ascontiguousarray(W.T))
    A_flatT = np.ascontiguousarray(A.reshape(256, D).T)
    A_T3 = np.ascontiguousarray(A_flatT.reshape(c.DC, 128, 256).transpose(1, 0, 2))
    Bf_er = np.ascontiguousarray(Bf.transpose(0, 2, 1).reshape(256, D))

    ident = np.eye(128, dtype=np.float32)
    NP = c.TCW // 128
    tri = np.zeros((max(NP, 1), 128, c.TCW), dtype=np.float32)
    for pp in range(NP):
        for k in range(128):
            tri[pp, k, : pp * 128 + k] = NEG
    E0 = np.zeros((128, 128), dtype=np.float32)
    E1 = np.zeros((128, 128), dtype=np.float32)
    for pp in range(128):
        E0[pp // 16, pp] = 1.0
        E1[8 + pp // 16, pp] = 1.0

    ins = []
    for core in range(8):
        b, me = core // 2, core % 2
        xb = np.ascontiguousarray(np.asarray(x)[b].T.astype(np.float32))
        rq = slice(H * 128 * me, H * 128 * (me + 1))
        rk = slice(D + H * 128 * me, D + H * 128 * (me + 1))
        rv = slice(2 * D + H * 128 * me, 2 * D + H * 128 * (me + 1))
        wq = np.concatenate([w_qkv[rq], w_qkv[rk], w_qkv[rv]], axis=0)
        wqkvT_blk = blocked(np.ascontiguousarray(wq.T))
        bq = np.concatenate([np.asarray(b_qkv)[rq], np.asarray(b_qkv)[rk],
                             np.asarray(b_qkv)[rv]]).astype(np.float32)
        woT = np.ascontiguousarray(w_out.T)
        order = np.concatenate([
            np.arange(H * 128 * me, H * 128 * (me + 1)),
            np.arange(H * 128 * (1 - me), H * 128 * (2 - me))])
        woutT_blk = blocked(np.ascontiguousarray(woT[order]))
        hs = slopes[H * me: H * me + H]
        al = np.zeros((128, H * c.KT), dtype=np.float32)
        for h in range(H):
            for kt in range(c.KT):
                al[:, h * c.KT + kt] = hs[h] * ((kt * 128 + np.arange(128)) - (T - 1))
        sh = np.zeros((H, 128, T), dtype=np.float32)
        qq = np.arange(T, dtype=np.float64)
        for h in range(H):
            sh[h, :, :] = (hs[h] * (T - 1 - qq) * SQ).astype(np.float32)[None, :]
        selT = np.zeros((128, 2), dtype=np.float32)
        selT[:, me] = 1.0
        selR = np.zeros((128, 4), dtype=np.float32)
        selR[:, 2 * me + (1 - me)] = 1.0
        d = dict(
            x_fm=xb,
            x_fm_own=np.ascontiguousarray(xb[:, c.OWN * me: c.OWN * (me + 1)]),
            wqkvT_blk=wqkvT_blk, bqkv=bq,
            woutT_blk=woutT_blk, bout=np.asarray(b_out).astype(np.float32),
            WT_blk=WT_blk, bffn=np.asarray(bffn).astype(np.float32),
            A_T3=A_T3, Bf_er=Bf_er,
            coeffs=np.asarray(coeffs).astype(np.float32),
            scales=np.asarray(scales).astype(np.float32),
            alibi_tbl=al, shift_tbl=sh, tri_neg=tri,
            E0=E0, E1=E1, ident_in=ident, selT=selT, selR=selR,
        )
        ins.append({k: np.ascontiguousarray(v) for k, v in d.items()})
    return ins



ENGN = ("tensor", "vector", "scalar", "sync", "gpsimd")
DMA_ENG = {"sync", "gpsimd"}


class Prog:
    def __init__(self, nc):
        self.nc = nc
        self.ops = {e: [] for e in ENGN}
        self.waited = {e: {} for e in ENGN}
        self.sems = {}
        for e in ENGN:
            self.sems[e] = [nc.alloc_semaphore(name=f"se_{e}"), 0]
        self.sems["cc"] = [nc.alloc_semaphore(name="se_cc"), 0]
        self.buf = {}

    def _sem_handle(self, name):
        return self.sems[name][0]

    def stream_sem(self, name):
        name = "ds_" + name
        if name not in self.sems:
            self.sems[name] = [self.nc.alloc_semaphore(name=name), 0]
        return name

    def op(self, eng, fn, reads=(), writes=(), extra=(), dma=False, cc=False,
           stream=None):
        waits = {}
        def add(sn, v):
            if v and waits.get(sn, 0) < v:
                waits[sn] = v
        for k in reads:
            b = self.buf.get(k)
            if b:
                for sn, v in b["w"].items():
                    add(sn, v)
        for k in writes:
            b = self.buf.get(k)
            if b:
                for sn, v in b["w"].items():
                    add(sn, v)
                for sn, v in b["r"].items():
                    add(sn, v)
        for sn, v in extra:
            add(sn, v)
        wl = []
        for sn, v in waits.items():
            if sn == eng == "tensor":
                continue
            if self.waited[eng].get(sn, -1) >= v:
                continue
            self.waited[eng][sn] = v
            wl.append((self._sem_handle(sn), v))
        if cc:
            sname = "cc"
        elif dma:
            assert stream is not None, "DMA ops need a stream name"
            sname = self.stream_sem(stream)
        else:
            sname = eng
        sem, val = self.sems[sname]
        k_inc = 16 if dma else 1
        val += k_inc
        self.sems[sname][1] = val
        mile = (sname, val)

        def run(e, wl=wl, fn=fn, sem=sem, k_inc=k_inc):
            for h, v in wl:
                e.wait_ge(h, v)
            ins = fn(e)
            ins.then_inc(sem, k_inc)
        self.ops[eng].append(run)
        for k in writes:
            b = self.buf.setdefault(k, {"w": {}, "r": {}})
            if mile[0] in b["w"] or (not dma and not cc):
                b["w"] = {mile[0]: mile[1]}
                b["r"] = {}
            else:
                b["w"][mile[0]] = mile[1]
                b["r"] = {}
        for k in reads:
            b = self.buf.setdefault(k, {"w": {}, "r": {}})
            if b["r"].get(mile[0], 0) < mile[1]:
                b["r"][mile[0]] = mile[1]
        return mile

    def wait_only(self, eng, extra):
        wl = []
        for sn, v in extra:
            if self.waited[eng].get(sn, -1) >= v:
                continue
            self.waited[eng][sn] = v
            wl.append((self._sem_handle(sn), v))
        if wl:
            def run(e, wl=wl):
                for h, v in wl:
                    e.wait_ge(h, v)
            self.ops[eng].append(run)

    def dma_fence(self):
        vals = {sn: v[1] for sn, v in self.sems.items() if sn not in ENGN}
        for e in ENGN:
            self.wait_only(e, [(sn, v) for sn, v in vals.items() if v])

    def barrier(self, skip=()):
        vals = {sn: self.sems[sn][1] for sn in self.sems if sn not in skip}
        for e in ENGN:
            self.wait_only(e, [(sn, v) for sn, v in vals.items() if sn != e])
        marks = {}
        for e in ENGN:
            marks[e] = self.op(e, self.mark_fns[e],
                               reads=["barz"], writes=[f"bar_{e}"])
        for e in ENGN:
            self.wait_only(e, [(o, marks[o][1]) for o in ENGN if o != e])

    def replay(self):
        nc = self.nc
        with nc.Block() as block:
            @block.tensor
            def _(t):
                for f in self.ops["tensor"]:
                    f(t)

            @block.vector
            def _(v):
                for f in self.ops["vector"]:
                    f(v)

            @block.scalar
            def _(s):
                for f in self.ops["scalar"]:
                    f(s)

            @block.sync
            def _(sy):
                for f in self.ops["sync"]:
                    f(sy)

            @block.gpsimd
            def _(g):
                for f in self.ops["gpsimd"]:
                    f(g)


def _nop(e):
    try:
        return e.engine_nop()
    except AttributeError:
        return e.nop()


def build_nc(cfg, dbg=(), phases=("A", "B", "X", "C")):
    c = cfg
    nc = bass.Bass("TRN2", target_bir_lowering=False, num_devices=8)
    T, D, H, F = c.T, c.D, c.H, c.F
    DC, FC, KT, OWN, TCW = c.DC, c.FC, c.KT, c.OWN, c.TCW
    NP = TCW // 128

    di = lambda n, s: nc.dram_tensor(n, s, F32, kind="ExternalInput")
    x_fm = di("x_fm", [D, T])
    x_fm_own = di("x_fm_own", [D, OWN])
    wqkvT_blk = di("wqkvT_blk", [FC, 128, DC, 128])
    bqkv = di("bqkv", [F])
    woutT_blk = di("woutT_blk", [DC, 128, DC, 128])
    bout = di("bout", [D])
    WT_blk = di("WT_blk", [DC, 128, DC, 128])
    bffn = di("bffn", [D])
    A_T3 = di("A_T3", [128, DC, 256])
    Bf_er = di("Bf_er", [256, D])
    coeffs = di("coeffs", [16])
    scales = di("scales", [16])
    alibi_tbl = di("alibi_tbl", [128, H * KT])
    shift_tbl = di("shift_tbl", [H, 128, T])
    tri_neg_i = di("tri_neg", [max(NP, 1), 128, TCW])
    E0_i = di("E0", [128, 128])
    E1_i = di("E1", [128, 128])
    ident_i = di("ident_in", [128, 128])
    selT_i = di("selT", [128, 2])
    selR_i = di("selR", [128, 4])

    out_e = nc.dram_tensor("out_e", [OWN, D], F32, kind="ExternalOutput")
    qkv_d = nc.dram_tensor("qkv_d", [F, T], BF16, kind="Internal")
    T01_d = nc.dram_tensor("T01_d", [2, H * 128, OWN], BF16, kind="Internal")
    R01_d = nc.dram_tensor("R01_d", [2, 2, H * 128, OWN], BF16, kind="Internal")
    attn1_d = nc.dram_tensor("attn1_d", [D, OWN], BF16, kind="Internal")

    dbg_t = {}
    if "qkv" in dbg:
        dbg_t["qkv"] = nc.dram_tensor("dbg_qkv", [F, T], BF16, kind="ExternalOutput")
    if "attn" in dbg:
        dbg_t["attn0"] = nc.dram_tensor("dbg_attn0", [H * 128, OWN], BF16,
                                        kind="ExternalOutput")
        dbg_t["attn1"] = nc.dram_tensor("dbg_attn1", [H * 128, OWN], BF16,
                                        kind="ExternalOutput")
    if "x1" in dbg:
        dbg_t["x1"] = nc.dram_tensor("dbg_x1", [D, OWN], BF16, kind="ExternalOutput")

    p = Prog(nc)
    top = contextlib.ExitStack()
    SB = lambda n, s, d: top.enter_context(nc.sbuf_tensor(n, s, d))
    bar_sb = SB("bar_sb", [1, 8], F32)
    bar_bf = SB("bar_bf", [1, 2], BF16)
    p.mark_fns = {
        "tensor": lambda e: e.ldweights(bar_bf[0:1, 0:1]),
        "vector": lambda e: e.memset(bar_sb[0:1, 1:2], 0.0),
        "scalar": lambda e: e.copy(bar_sb[0:1, 2:3], bar_sb[0:1, 0:1]),
        "sync": _nop,
        "gpsimd": _nop,
    }

    ones_c_bf = SB("ones_c_bf", [128, 1], BF16)
    ones_c_f = SB("ones_c_f", [128, 1], F32)
    ones_r_f = SB("ones_r_f", [1, 128], F32)
    ones_row_bf = SB("ones_row_bf", [1, TCW], BF16)
    ident = SB("ident", [128, 128], F32)
    sel_sb = SB("sel_sb", [128, 6], F32)
    bout_sb = SB("bout_sb", [128, DC], F32)
    rowsS = SB("rowsS", [1, 4 * TCW], F32)
    bqb2 = SB("bqb2", [1, F], BF16)
    wcs_bf = SB("wcs_bf", [1, F], BF16)
    negmS = SB("negmS", [1, max(c.TCH, OWN)], BF16)
    rinv_bf = SB("rinv_bf", [1, c.TCH], BF16)
    rbcS = SB("rbcS", [128, max(c.TCH, OWN)], F32)
    acol_bf = SB("acol_bf", [1, 256], BF16)
    wexp_sb = SB("wexp_sb", [128, 2], F32)
    wcs2 = SB("wcs2", [1, 128], BF16)
    we_tmp = SB("we_tmp", [128, 4], F32)
    Epad = SB("Epad", [128, 256], F32)

    G, SY, V, S, TE = "gpsimd", "sync", "vector", "scalar", "tensor"

    p.op(G, lambda e: e.dma_start(out=ident[:, :], in_=ident_i[:, :]),
         writes=["ident"], dma=True, stream="init")
    p.op(G, lambda e: e.dma_start(out=sel_sb[:, 0:2], in_=selT_i[:, :]),
         writes=["selT"], dma=True, stream="init")
    p.op(G, lambda e: e.dma_start(out=sel_sb[:, 2:6], in_=selR_i[:, :]),
         writes=["selR"], dma=True, stream="init")
    for oc in range(DC):
        p.op(G, lambda e, oc=oc: e.dma_start(
            out=bout_sb[:, oc:oc + 1],
            in_=bout.ap()[oc * 128:(oc + 1) * 128].rearrange("(p one) -> p one", one=1)),
            writes=["boutc"], dma=True, stream="init")

    p.op(V, lambda e: e.memset(bar_sb[0:1, :], 0.0), writes=["barz"])
    p.op(V, lambda e: e.memset(bar_bf[0:1, :], 1.0), writes=["barz"])
    p.op(V, lambda e: e.memset(ones_c_bf[:, :], 1.0), writes=["ones_c"])
    p.op(V, lambda e: e.memset(ones_c_f[:, :], 1.0), writes=["ones_cf"])
    p.op(V, lambda e: e.memset(ones_r_f[:, :], 1.0), writes=["ones_r"])
    p.op(V, lambda e: e.memset(ones_row_bf[:, :], 1.0), writes=["ones_row"])

    p.barrier()

    with contextlib.ExitStack() as sa:
      if "A" in phases:
        SBa = lambda n, s, d: sa.enter_context(nc.sbuf_tensor(n, s, d))
        PSa = lambda n, s: sa.enter_context(nc.psum_tensor(n, s, F32))
        TCH, NTC, NCH = c.TCH, c.NTC, c.NCH
        xch = [SBa(f"xch{i}", [128, DC * TCH], BF16) for i in range(2)]
        wcol = [SBa(f"wcol{i}", [128, DC * 128], BF16) for i in range(2)]
        qo = [SBa(f"qo{i}", [128, TCH], BF16) for i in range(2)]
        sq4 = SBa("sq4", [128, 2 * TCH], BF16)
        rows = rowsS
        negm_bf = negmS
        rbc = rbcS
        bq_bf = bqb2
        pm = [PSa(f"pmA{i}", [128, NTC * PW]) for i in range(2)]
        prb = PSa("prbA", [128, NTC * PW])
        p.op(G, lambda e: e.dma_start(
            out=bq_bf[0:1, :], in_=bqkv.ap().rearrange("(one f) -> one f", one=1)),
            writes=["bq"], dma=True, stream="init")
        p.dma_fence()

        for ch in range(NCH):
            par = ch % 2
            XK = f"xch{par}"
            for dc in range(DC):
                p.op(G, lambda e, dc=dc, ch=ch, par=par: e.dma_start(
                    out=xch[par][:, dc * TCH:(dc + 1) * TCH],
                    in_=x_fm[dc * 128:(dc + 1) * 128, ch * TCH:(ch + 1) * TCH]),
                    writes=[XK], dma=True, stream=f"xc{par}")

            for grp in range(DC // 2):
                for j in range(2):
                    dc = grp * 2 + j
                    p.op(S, lambda e, dc=dc, j=j, par=par: e.activation(
                        sq4[:, j * TCH:(j + 1) * TCH],
                        xch[par][:, dc * TCH:(dc + 1) * TCH], AF.Square),
                        reads=[XK], writes=[f"sq{j}"])
                for j in range(2):
                    dc = grp * 2 + j
                    st, sp = (dc == 0), (dc == DC - 1)
                    for tc in range(NTC):
                        p.op(TE, lambda e, dc=dc, tc=tc, par=par, st=st, sp=sp: e.matmul(
                            pm[0][0:1, tc * PW: tc * PW + TCW], ones_c_bf[:, :],
                            xch[par][:, dc * TCH + tc * TCW: dc * TCH + (tc + 1) * TCW],
                            start=st, stop=sp),
                            reads=[XK, "ones_c"], writes=["pm0"])
                        p.op(TE, lambda e, j=j, tc=tc, st=st, sp=sp: e.matmul(
                            pm[1][0:1, tc * PW: tc * PW + TCW], ones_c_bf[:, :],
                            sq4[:, j * TCH + tc * TCW: j * TCH + (tc + 1) * TCW],
                            start=st, stop=sp),
                            reads=[f"sq{j}", "ones_c"], writes=["pm1"])
            R = rows
            for tc in range(NTC):
                o = tc * TCW
                p.op(V, lambda e, tc=tc: e.tensor_scalar(
                    R[0:1, 0:TCW], pm[0][0:1, tc * PW: tc * PW + TCW],
                    1.0 / D, None, ALU.mult), reads=["pm0"], writes=["rows"])
                p.op(V, lambda e, tc=tc: e.tensor_scalar(
                    R[0:1, TCW:2 * TCW], pm[1][0:1, tc * PW: tc * PW + TCW],
                    1.0 / D, None, ALU.mult), reads=["pm1"], writes=["rows"])
                p.op(V, lambda e: e.tensor_mul(
                    R[0:1, 2 * TCW:3 * TCW], R[0:1, 0:TCW], R[0:1, 0:TCW]),
                    reads=["rows"], writes=["rows"])
                p.op(V, lambda e: e.tensor_sub(
                    R[0:1, 2 * TCW:3 * TCW], R[0:1, TCW:2 * TCW], R[0:1, 2 * TCW:3 * TCW]),
                    reads=["rows"], writes=["rows"])
                p.op(V, lambda e: e.tensor_scalar(
                    R[0:1, 2 * TCW:3 * TCW], R[0:1, 2 * TCW:3 * TCW], EPS, None,
                    ALU.add), reads=["rows"], writes=["rows"])
                p.op(S, lambda e: e.activation(
                    R[0:1, TCW:2 * TCW], R[0:1, 2 * TCW:3 * TCW], AF.Ln),
                    reads=["rows"], writes=["rows"])
                p.op(S, lambda e: e.activation(
                    R[0:1, 3 * TCW:4 * TCW], R[0:1, TCW:2 * TCW], AF.Exp, scale=-0.5),
                    reads=["rows"], writes=["rows"])
                p.op(S, lambda e, o=o: e.activation(
                    rinv_bf[0:1, o:o + TCW], R[0:1, TCW:2 * TCW], AF.Exp, scale=0.5),
                    reads=["rows"], writes=["rinv"])
                p.op(V, lambda e, o=o: e.tensor_scalar(
                    negm_bf[0:1, o:o + TCW], R[0:1, 0:TCW], -1.0, None, ALU.mult),
                    reads=["rows"], writes=["negm"])
                p.op(TE, lambda e, tc=tc: e.matmul(
                    prb[:, tc * PW: tc * PW + TCW], ones_r_f[:, :],
                    R[0:1, 3 * TCW:4 * TCW], start=True, stop=True),
                    reads=["rows", "ones_r"], writes=["prb"])
                p.op(V, lambda e, o=o, tc=tc: e.tensor_copy(
                    rbc[:, o:o + TCW], prb[:, tc * PW: tc * PW + TCW]),
                    reads=["prb"], writes=["rbc"])

            for fc in range(FC):
                gfc = ch * FC + fc
                wpar = gfc % 2
                WK = f"wcol{wpar}"
                p.op(G, lambda e, fc=fc, wpar=wpar: e.dma_start(
                    out=wcol[wpar].ap().rearrange("p (dc f) -> p dc f", f=128),
                    in_=wqkvT_blk[fc, :, :, :]),
                    writes=[WK], dma=True, stream=f"wl{wpar}")
                if ch == 0:
                    for dc in range(DC):
                        p.op(TE, lambda e, dc=dc, wpar=wpar: e.matmul(
                            prb[0:1, 0:128], ones_c_bf[:, :],
                            wcol[wpar][:, dc * 128:(dc + 1) * 128],
                            start=(dc == 0), stop=(dc == DC - 1)),
                            reads=[WK, "ones_c"], writes=["prb"])
                    p.op(V, lambda e, fc=fc: e.tensor_copy(
                        wcs_bf[0:1, fc * 128:(fc + 1) * 128], prb[0:1, 0:128]),
                        reads=["prb"], writes=["wcs"])
                ppar = gfc % 2
                PK = f"pm{ppar}"
                for dc in range(DC):
                    for tc in range(NTC):
                        p.op(TE, lambda e, dc=dc, tc=tc, wpar=wpar, ppar=ppar, par=par:
                             e.matmul(
                                 pm[ppar][:, tc * PW: tc * PW + TCW],
                                 wcol[wpar][:, dc * 128:(dc + 1) * 128],
                                 xch[par][:, dc * TCH + tc * TCW: dc * TCH + (tc + 1) * TCW],
                                 start=(dc == 0), stop=False),
                             reads=[WK, XK], writes=[PK])
                for tc in range(NTC):
                    p.op(TE, lambda e, fc=fc, tc=tc, ppar=ppar: e.matmul(
                        pm[ppar][:, tc * PW: tc * PW + TCW],
                        wcs_bf[0:1, fc * 128:(fc + 1) * 128],
                        negm_bf[0:1, tc * TCW:(tc + 1) * TCW], start=False, stop=False),
                        reads=["wcs", "negm"], writes=[PK])
                    p.op(TE, lambda e, fc=fc, tc=tc, ppar=ppar: e.matmul(
                        pm[ppar][:, tc * PW: tc * PW + TCW],
                        bq_bf[0:1, fc * 128:(fc + 1) * 128],
                        rinv_bf[0:1, tc * TCW:(tc + 1) * TCW], start=False, stop=True),
                        reads=["bq", "rinv"], writes=[PK])
                QK = f"qo{ppar}"
                for tc in range(NTC):
                    p.op(V, lambda e, tc=tc, ppar=ppar: e.tensor_tensor(
                        qo[ppar][:, tc * TCW:(tc + 1) * TCW],
                        pm[ppar][:, tc * PW: tc * PW + TCW],
                        rbc[:, tc * TCW:(tc + 1) * TCW], ALU.mult),
                        reads=[PK, "rbc"], writes=[QK] if tc == 0 else [QK])
                p.op(SY, lambda e, fc=fc, ch=ch, ppar=ppar: e.dma_start(
                    out=qkv_d[fc * 128:(fc + 1) * 128, ch * TCH:(ch + 1) * TCH],
                    in_=qo[ppar][:, :]),
                    reads=[QK], writes=["qkv_d"], dma=True, stream=f"qw{ppar}")
        if "qkv" in dbg:
            p.op(SY, lambda e: e.dma_start(out=dbg_t["qkv"][:, :], in_=qkv_d[:, :]),
                 reads=["qkv_d"], writes=["dbgq"], dma=True, stream="dbg")
    p.barrier()

    with contextlib.ExitStack() as sb2:
      if "B" in phases:
        SBb = lambda n, s, d: sb2.enter_context(nc.sbuf_tensor(n, s, d))
        PSb = lambda n, s: sb2.enter_context(nc.psum_tensor(n, s, F32))
        QCPW, NQCP, QKTW, NTW = c.QCPW, c.NQCP, c.QCPW // 128, c.NTW
        WIN = 8 if KT > 8 else KT
        al_sb = SBb("al_sb", [128, H * KT], F32)
        tri_sb = SBb("tri_sb", [128, max(NP, 1) * TCW], F32)
        p.op(G, lambda e: e.dma_start(out=al_sb[:, :], in_=alibi_tbl[:, :]),
             writes=["al"], dma=True, stream="init")
        for pp_ in range(max(NP, 1)):
            p.op(G, lambda e, pp_=pp_: e.dma_start(
                out=tri_sb[:, pp_ * TCW:(pp_ + 1) * TCW], in_=tri_neg_i[pp_, :, :]),
                writes=["tri"], dma=True, stream="init")
        p.dma_fence()
        q_sb = [SBb(f"q_sb{i}", [128, T], BF16) for i in range(2)]
        k_sb = [SBb(f"k_sb{i}", [128, T], BF16) for i in range(2)]
        v_tm = [SBb(f"v_tm{i}", [128, KT * 128], BF16) for i in range(2)]
        PT = SBb("PT", [128, 2 * WIN * QCPW], BF16)
        den = [SBb(f"denB{i}", [128, QCPW], F32) for i in range(2)]
        rb2 = [SBb(f"rb2{i}", [128, QCPW], F32) for i in range(2)]
        sh_sb = [SBb(f"sh_sb{i}", [128, QCPW], F32) for i in range(2)]
        att = [SBb(f"att{i}", [128, QCPW], BF16) for i in range(2)]
        ps = [PSb(f"psB{i}", [128, NTW * PW]) for i in range(2)]
        po = [PSb(f"poB{i}", [128, NTW * PW]) for i in range(2)]

        def pt_slot(cn, kt):
            return (cn * WIN + (kt % WIN)) * QCPW

        for hp in range(H // 2):
            heads = (2 * hp, 2 * hp + 1)
            for cn, h in enumerate(heads):
                QH, KH, VH = f"qh{cn}", f"kh{cn}", f"vh{cn}"
                p.op(SY, lambda e, h=h, cn=cn: e.dma_start(
                    out=q_sb[cn][:, :], in_=qkv_d[h * 128:(h + 1) * 128, :]),
                    reads=["qkv_d"], writes=[QH], dma=True, stream=f"ql{cn}")
                p.op(SY, lambda e, h=h, cn=cn: e.dma_start(
                    out=k_sb[cn][:, :],
                    in_=qkv_d[H * 128 + h * 128: H * 128 + (h + 1) * 128, :]),
                    reads=["qkv_d"], writes=[KH], dma=True, stream=f"kl{cn}")
                for kt in range(KT):
                    p.op(SY, lambda e, h=h, cn=cn, kt=kt: e.dma_start(
                        out=v_tm[cn][:, kt * 128:(kt + 1) * 128],
                        in_=qkv_d[2 * H * 128 + h * 128: 2 * H * 128 + (h + 1) * 128,
                                  kt * 128:(kt + 1) * 128],
                        transpose=True),
                        reads=["qkv_d"], writes=[VH], dma=True, stream=f"vl{cn}")

            for qcp in range(c.NQCP):
                kmax = QKTW * (qcp + 1)
                for cn, h in enumerate(heads):
                    p.op(SY, lambda e, h=h, qcp=qcp, cn=cn: e.dma_start(
                        out=sh_sb[cn][:, :],
                        in_=shift_tbl[h, :, qcp * QCPW:(qcp + 1) * QCPW]),
                        writes=[f"sh{cn}"], dma=True, stream=f"shl{cn}")

                def pv_mm(cn, kt):
                    st, sp = (kt == 0), (kt == kmax - 1)
                    VH = f"vh{cn}"
                    for tw in range(NTW):
                        p.op(TE, lambda e, cn=cn, kt=kt, tw=tw, st=st, sp=sp: e.matmul(
                            po[cn][:, tw * PW: tw * PW + TCW],
                            v_tm[cn][:, kt * 128:(kt + 1) * 128],
                            PT[:, pt_slot(cn, kt) + tw * TCW:
                                  pt_slot(cn, kt) + (tw + 1) * TCW],
                            start=st, stop=sp),
                            reads=[VH, f"PT{cn}"], writes=[f"po{cn}"])

                def den_acc(cn, kt):
                    if kt == 0:
                        p.op(G, lambda e, cn=cn, kt=kt: e.tensor_copy(
                            den[cn][:, :],
                            PT[:, pt_slot(cn, kt): pt_slot(cn, kt) + QCPW]),
                            reads=[f"PT{cn}"], writes=[f"den{cn}"])
                    else:
                        p.op(G, lambda e, cn=cn, kt=kt: e.tensor_tensor(
                            den[cn][:, :], den[cn][:, :],
                            PT[:, pt_slot(cn, kt): pt_slot(cn, kt) + QCPW],
                            ALU.add), reads=[f"PT{cn}"], writes=[f"den{cn}"])

                for kt in range(kmax):
                    for cn, h in enumerate(heads):
                        QH, KH = f"qh{cn}", f"kh{cn}"
                        PSK = f"psB{cn}"
                        for tw in range(NTW):
                            p.op(TE, lambda e, h=h, qcp=qcp, kt=kt, tw=tw, cn=cn:
                                 e.matmul(
                                     ps[cn][:, tw * PW: tw * PW + TCW],
                                     k_sb[cn][:, kt * 128:(kt + 1) * 128],
                                     q_sb[cn][:, qcp * QCPW + tw * TCW:
                                               qcp * QCPW + (tw + 1) * TCW],
                                     start=True, stop=True),
                                 reads=[KH, QH], writes=[PSK])
                        if kt >= 2:
                            pv_mm(cn, kt - 2)
                        if TCW == PW:
                            p.op(V, lambda e, cn=cn: e.tensor_tensor(
                                ps[cn][:, 0:NTW * PW], ps[cn][:, 0:NTW * PW],
                                sh_sb[cn][:, :], ALU.add),
                                reads=[f"sh{cn}", PSK], writes=[PSK])
                        else:
                            for tw in range(NTW):
                                p.op(V, lambda e, cn=cn, tw=tw: e.tensor_tensor(
                                    ps[cn][:, tw * PW: tw * PW + TCW],
                                    ps[cn][:, tw * PW: tw * PW + TCW],
                                    sh_sb[cn][:, tw * TCW:(tw + 1) * TCW], ALU.add),
                                    reads=[f"sh{cn}", PSK], writes=[PSK])
                        if kt >= kmax - QKTW:
                            tw = (kt * 128 - qcp * QCPW) // TCW
                            pos = (kt * 128 - qcp * QCPW - tw * TCW) // 128
                            p.op(V, lambda e, cn=cn, tw=tw, pos=pos: e.tensor_tensor(
                                ps[cn][:, tw * PW: tw * PW + TCW],
                                ps[cn][:, tw * PW: tw * PW + TCW],
                                tri_sb[:, pos * TCW:(pos + 1) * TCW], ALU.add),
                                reads=["tri", PSK], writes=[PSK])
                            for twf in range(tw):
                                p.op(V, lambda e, cn=cn, twf=twf: e.tensor_scalar(
                                    ps[cn][:, twf * PW: twf * PW + TCW],
                                    ps[cn][:, twf * PW: twf * PW + TCW],
                                    NEG, None, ALU.add),
                                    reads=[PSK], writes=[PSK])
                        if TCW == PW:
                            p.op(S, lambda e, h=h, kt=kt, cn=cn: e.activation(
                                PT[:, pt_slot(cn, kt): pt_slot(cn, kt) + QCPW],
                                ps[cn][:, 0:NTW * PW], AF.Exp,
                                bias=al_sb[:, h * KT + kt: h * KT + kt + 1],
                                scale=ISQ),
                                reads=[PSK, "al"], writes=[f"PT{cn}"])
                        else:
                            for tw in range(NTW):
                                p.op(S, lambda e, h=h, kt=kt, cn=cn, tw=tw:
                                     e.activation(
                                         PT[:, pt_slot(cn, kt) + tw * TCW:
                                               pt_slot(cn, kt) + (tw + 1) * TCW],
                                         ps[cn][:, tw * PW: tw * PW + TCW], AF.Exp,
                                         bias=al_sb[:, h * KT + kt: h * KT + kt + 1],
                                         scale=ISQ),
                                     reads=[PSK, "al"], writes=[f"PT{cn}"])
                        if kt >= 1:
                            den_acc(cn, kt - 1)
                for cn, h in enumerate(heads):
                    for kt in range(max(kmax - 2, 0), kmax):
                        pv_mm(cn, kt)
                    den_acc(cn, kmax - 1)
                for cn, h in enumerate(heads):
                    for tw in range(NTW):
                        p.op(TE, lambda e, cn=cn, tw=tw: e.matmul(
                            ps[cn][0:1, tw * PW: tw * PW + TCW], ones_c_f[:, :],
                            den[cn][:, tw * TCW:(tw + 1) * TCW],
                            start=True, stop=True),
                            reads=[f"den{cn}", "ones_cf"], writes=[f"psB{cn}"])
                    for tw in range(NTW):
                        p.op(V, lambda e, cn=cn, tw=tw: e.reciprocal(
                            rb2[cn][0:1, tw * TCW:(tw + 1) * TCW],
                            ps[cn][0:1, tw * PW: tw * PW + TCW]),
                            reads=[f"psB{cn}"], writes=[f"rd{cn}"])
                    for tw in range(NTW):
                        p.op(TE, lambda e, cn=cn, tw=tw: e.matmul(
                            ps[cn][:, tw * PW: tw * PW + TCW], ones_r_f[:, :],
                            rb2[cn][0:1, tw * TCW:(tw + 1) * TCW],
                            start=True, stop=True),
                            reads=[f"rd{cn}", "ones_r"], writes=[f"psB{cn}"])
                    for tw in range(NTW):
                        p.op(V, lambda e, cn=cn, tw=tw: e.tensor_copy(
                            rb2[cn][:, tw * TCW:(tw + 1) * TCW],
                            ps[cn][:, tw * PW: tw * PW + TCW]),
                            reads=[f"psB{cn}"], writes=[f"rb2{cn}", f"rd{cn}"])
                    for tw in range(NTW):
                        p.op(V, lambda e, cn=cn, tw=tw: e.tensor_tensor(
                            att[cn][:, tw * TCW:(tw + 1) * TCW],
                            po[cn][:, tw * PW: tw * PW + TCW],
                            rb2[cn][:, tw * TCW:(tw + 1) * TCW], ALU.mult),
                            reads=[f"po{cn}", f"rb2{cn}"], writes=[f"att{cn}"])
                    half = (qcp * QCPW) // OWN
                    off = qcp * QCPW - half * OWN
                    p.op(SY, lambda e, h=h, cn=cn, half=half, off=off: e.dma_start(
                        out=T01_d[half, h * 128:(h + 1) * 128, off:off + QCPW],
                        in_=att[cn][:, :]),
                        reads=[f"att{cn}"], writes=[f"T{half}"], dma=True,
                        stream=f"aw{cn}")
        if "attn" in dbg:
            p.op(SY, lambda e: e.dma_start(out=dbg_t["attn0"][:, :],
                                           in_=T01_d[0, :, :]),
                 reads=["T0"], writes=["dbga0"], dma=True, stream="dbg")
            p.op(SY, lambda e: e.dma_start(out=dbg_t["attn1"][:, :],
                                           in_=T01_d[1, :, :]),
                 reads=["T1"], writes=["dbga1"], dma=True, stream="dbg")
    p.barrier()

    if "X" in phases:
      p.op(G, lambda e: e.collective_compute(
        "AllGather", ALU.bypass, replica_groups=[[0, 1], [2, 3], [4, 5], [6, 7]],
        ins=[T01_d.ap().opt()], outs=[R01_d.ap().opt()]),
        reads=["T0", "T1"], writes=["R0", "R1"], cc=True)
    p.barrier(skip=("cc",))

    with contextlib.ExitStack() as sc:
      if "C" in phases:
        SBc = lambda n, s, d: sc.enter_context(nc.sbuf_tensor(n, s, d))
        PSc = lambda n, s: sc.enter_context(nc.psum_tensor(n, s, F32))
        CS = getattr(c, "CSPLIT", 2 if OWN >= 2048 else 1)
        OWNC = OWN // CS
        W2 = min(c.QCPW, OWNC)
        NT2 = OWNC // W2
        NW2 = W2 // TCW
        NOC = OWNC // TCW

        BB = SBc("BB", [128, DC * OWNC], BF16)
        cbuf = SBc("cbuf", [128, 2 * W2], BF16)
        cnd = [cbuf.ap()[:, 0:W2], cbuf.ap()[:, W2:2 * W2]]
        wco = [SBc(f"wco{i}", [128, DC * 128], BF16) for i in range(2)]
        a1s = [cbuf.ap()[:, 0:W2], cbuf.ap()[:, W2:2 * W2]]
        sq2 = SBc("sq2", [128, 2 * TCW], BF16)
        rows2 = rowsS
        negm2 = negmS
        rbc2 = rbcS
        xa_sb = SBc("xa_sb", [128, 2 * OWNC], BF16)
        b2_bf = bqb2
        Bfw_sb = SBc("Bfw_sb", [128, 2 * D], BF16)
        A_sb = SBc("A_sb", [128, DC * 256], BF16)
        xo1 = [SBc(f"xo1{i}", [128, W2], F32) for i in range(2)]
        a1c = [SBc(f"a1c{i}", [128, W2], BF16) for i in range(2)]
        tmp2 = [SBc(f"tmp2{i}", [128, W2], F32) for i in range(2)]
        fin = [SBc(f"fin{i}", [128, W2], F32) for i in range(2)]
        tb = [SBc(f"tb{i}", [128, 512], F32) for i in range(2)]
        pm2 = [PSc(f"pm2{i}", [128, NW2 * PW]) for i in range(2)]
        ptb = [PSc(f"ptb{i}", [128, 512]) for i in range(2)]

        p.op(G, lambda e: e.dma_start(
            out=b2_bf[0:1, 0:D], in_=bffn.ap().rearrange("(one f) -> one f", one=1)),
            writes=["b2"], dma=True, stream="init")
        p.op(G, lambda e: e.dma_start(
            out=A_sb.ap().rearrange("p (dc er) -> p dc er", er=256),
            in_=A_T3[:, :, :]),
            writes=["A"], dma=True, stream="Ainit")
        p.op(G, lambda e: e.dma_start(out=Epad[:, 0:128], in_=E0_i[:, :]),
             writes=["Epad"], dma=True, stream="epad0")
        p.op(G, lambda e: e.dma_start(out=Epad[:, 128:256], in_=E1_i[:, :]),
             writes=["Epad2"], dma=True, stream="epad1")
        p.op(V, lambda e: e.memset(we_tmp[:, :], 0.0), writes=["we"])
        p.op(G, lambda e: e.dma_start(
            out=we_tmp[0:16, 0:1],
            in_=coeffs.ap().rearrange("(p one) -> p one", one=1)),
            reads=["we"], writes=["we_c"], dma=True, stream="wec")
        p.op(G, lambda e: e.dma_start(
            out=we_tmp[0:16, 1:2],
            in_=scales.ap().rearrange("(p one) -> p one", one=1)),
            reads=["we"], writes=["we_s"], dma=True, stream="wes")
        p.op(V, lambda e: e.tensor_mul(we_tmp[:, 2:3], we_tmp[:, 0:1],
                                       we_tmp[:, 1:2]),
             reads=["we", "we_c", "we_s"], writes=["we_p"])
        p.op(TE, lambda e: e.matmul(pm2[0][:, 0:1], Epad[:, 0:128], we_tmp[:, 2:3],
                                    start=True, stop=True),
             reads=["Epad", "we_p"], writes=["pm20"])
        p.op(TE, lambda e: e.matmul(pm2[0][:, 1:2], Epad[:, 128:256], we_tmp[:, 2:3],
                                    start=True, stop=True),
             reads=["Epad2", "we_p"], writes=["pm20"])
        p.op(V, lambda e: e.tensor_copy(wexp_sb[:, :], pm2[0][:, 0:2]),
             reads=["pm20"], writes=["wexp"])
        NBF = D // W2
        for erc in range(2):
            for bchunk in range(NBF):
                p.op(G, lambda e, erc=erc, bchunk=bchunk: e.dma_start(
                    out=xo1[0][:, :],
                    in_=Bf_er[erc * 128:(erc + 1) * 128,
                              bchunk * W2:(bchunk + 1) * W2]),
                    reads=["bfs_r"], writes=["bfs"], dma=True, stream="bfs")
                p.op(S, lambda e, erc=erc, bchunk=bchunk: e.activation(
                    Bfw_sb[:, erc * D + bchunk * W2: erc * D + (bchunk + 1) * W2],
                    xo1[0][:, :], AF.Copy,
                    bias=0.0, scale=wexp_sb[:, erc:erc + 1]),
                    reads=["bfs", "wexp"], writes=[f"Bfw{erc}", "bfs_r"])
        for erc in range(2):
            for dc in range(DC):
                p.op(TE, lambda e, erc=erc, dc=dc: e.matmul(
                    pm2[0][0:1, 256 + erc * 128:384 + erc * 128],
                    ones_c_bf[:, :],
                    A_sb[:, dc * 256 + erc * 128: dc * 256 + erc * 128 + 128],
                    start=(dc == 0), stop=(dc == DC - 1)),
                    reads=["A", "ones_c"], writes=["pm20"])
        p.op(V, lambda e: e.tensor_copy(acol_bf[0:1, :], pm2[0][0:1, 256:512]),
             reads=["pm20"], writes=["acol"])
        p.dma_fence()

        for chalf in range(CS):
          TB = chalf * OWNC

          for fc in range(DC):
            for hw in range(NT2):
                sl = slice(TB + hw * W2, TB + (hw + 1) * W2)
                bsl = BB[:, fc * OWNC + hw * W2: fc * OWNC + (hw + 1) * W2]
                if fc < DC // 2:
                    srcs = [(T01_d[0], fc, "T0"), (T01_d[1], fc, "T1")]
                    coefs = [sel_sb[:, 0:1], sel_sb[:, 1:2]]
                    selk = ["selT", "selT"]
                else:
                    j = fc - DC // 2
                    srcs = [(R01_d[0, 0], j, "R0"), (R01_d[1, 0], j, "R0"),
                            (R01_d[0, 1], j, "R1"), (R01_d[1, 1], j, "R1")]
                    coefs = [sel_sb[:, 2:3], sel_sb[:, 3:4],
                             sel_sb[:, 4:5], sel_sb[:, 5:6]]
                    selk = ["selR"] * 4
                BK = "BBo" if fc < DC // 2 else "BBp"
                for i, (src, r, kk) in enumerate(srcs):
                    ci = i % 2
                    p.op(S, lambda e, src=src, r=r, sl=sl, ci=ci: e.dma_start(
                        out=cnd[ci][:, :], in_=src[r * 128:(r + 1) * 128, sl]),
                        reads=[kk], writes=[f"cnd{ci}"], dma=True, stream=f"cn{ci}")
                    if i == 0:
                        p.op(V, lambda e, ci=ci, bsl=bsl, cf=coefs[i]: e.tensor_scalar(
                            bsl, cnd[ci][:, :], cf, None, ALU.mult),
                            reads=[f"cnd{ci}", selk[i]], writes=[BK])
                    else:
                        p.op(V, lambda e, ci=ci, bsl=bsl, cf=coefs[i]:
                             e.scalar_tensor_tensor(
                                 bsl, cnd[ci][:, :], cf, bsl, ALU.mult, ALU.add),
                             reads=[f"cnd{ci}", selk[i], BK], writes=[BK])

          for oc in range(DC):
            wpar = oc % 2
            WK = f"wco{wpar}"
            p.op(G, lambda e, oc=oc, wpar=wpar: e.dma_start(
                out=wco[wpar].ap().rearrange("p (dc f) -> p dc f", f=128),
                in_=woutT_blk[oc, :, :, :]),
                writes=[WK], dma=True, stream=f"wc{wpar}")
            for tcp in range(NT2):
                ppar = (oc * NT2 + tcp) % 2
                PK = f"pm2{ppar}"
                for dc in range(DC):
                    for tw in range(NW2):
                        p.op(TE, lambda e, dc=dc, tcp=tcp, tw=tw, wpar=wpar, ppar=ppar:
                             e.matmul(
                                 pm2[ppar][:, tw * PW: tw * PW + TCW],
                                 wco[wpar][:, dc * 128:(dc + 1) * 128],
                                 BB[:, dc * OWNC + tcp * W2 + tw * TCW:
                                       dc * OWNC + tcp * W2 + (tw + 1) * TCW],
                                 start=(dc == 0), stop=(dc == DC - 1)),
                             reads=[WK, "BBo" if dc < DC // 2 else "BBp"],
                             writes=[PK])
                ab = ppar
                for tw in range(NW2):
                    p.op(S, lambda e, oc=oc, ab=ab, ppar=ppar, tw=tw: e.activation(
                        a1s[ab][:, tw * TCW:(tw + 1) * TCW],
                        pm2[ppar][:, tw * PW: tw * PW + TCW], AF.Identity,
                        bias=bout_sb[:, oc:oc + 1]),
                        reads=[PK, "boutc"], writes=[f"a1s{ab}"])
                p.op(SY, lambda e, oc=oc, tcp=tcp, ab=ab, TB=TB: e.dma_start(
                    out=attn1_d[oc * 128:(oc + 1) * 128,
                                TB + tcp * W2: TB + (tcp + 1) * W2],
                    in_=a1s[ab][:, :]),
                    reads=[f"a1s{ab}"], writes=["attn1"], dma=True, stream=f"a1w{ab}")

          for dc in range(DC):
            for tcp in range(NT2):
                bpar = (dc * NT2 + tcp) % 2
                p.op(S, lambda e, dc=dc, tcp=tcp, bpar=bpar, TB=TB: e.dma_start(
                    out=xo1[bpar][:, :],
                    in_=x_fm_own[dc * 128:(dc + 1) * 128,
                                 TB + tcp * W2: TB + (tcp + 1) * W2]),
                    writes=[f"xo{bpar}"], dma=True, stream=f"xol{bpar}")
                p.op(S, lambda e, dc=dc, tcp=tcp, bpar=bpar, TB=TB: e.dma_start(
                    out=a1c[bpar][:, :],
                    in_=attn1_d[dc * 128:(dc + 1) * 128,
                                TB + tcp * W2: TB + (tcp + 1) * W2]),
                    reads=["attn1"], writes=[f"a1c{bpar}"], dma=True,
                    stream=f"a1l{bpar}")
                p.op(V, lambda e, dc=dc, tcp=tcp, bpar=bpar: e.tensor_add(
                    BB[:, dc * OWNC + tcp * W2: dc * OWNC + (tcp + 1) * W2],
                    xo1[bpar][:, :], a1c[bpar][:, :]),
                    reads=[f"xo{bpar}", f"a1c{bpar}"],
                    writes=["BBo" if dc < DC // 2 else "BBp"])
          if "x1" in dbg:
            for dc in range(DC):
                p.op(SY, lambda e, dc=dc, TB=TB: e.dma_start(
                    out=dbg_t["x1"][dc * 128:(dc + 1) * 128, TB:TB + OWNC],
                    in_=BB[:, dc * OWNC:(dc + 1) * OWNC]),
                    reads=["BBo", "BBp"], writes=["dbgx1"], dma=True, stream="dbg")

          for tc in range(NOC):
            for grp in range(DC // 2):
                for j in range(2):
                    dc = grp * 2 + j
                    p.op(S, lambda e, dc=dc, j=j, tc=tc: e.activation(
                        sq2[:, j * TCW:(j + 1) * TCW],
                        BB[:, dc * OWNC + tc * TCW: dc * OWNC + (tc + 1) * TCW],
                        AF.Square), reads=["BBo", "BBp"], writes=[f"sqc{j}"])
                for j in range(2):
                    dc = grp * 2 + j
                    st, sp = (dc == 0), (dc == DC - 1)
                    p.op(TE, lambda e, dc=dc, tc=tc, st=st, sp=sp: e.matmul(
                        pm2[0][0:1, 0:TCW], ones_c_bf[:, :],
                        BB[:, dc * OWNC + tc * TCW: dc * OWNC + (tc + 1) * TCW],
                        start=st, stop=sp), reads=["BBo", "BBp", "ones_c"], writes=["pm20"])
                    p.op(TE, lambda e, j=j, st=st, sp=sp: e.matmul(
                        pm2[1][0:1, 0:TCW], ones_c_bf[:, :],
                        sq2[:, j * TCW:(j + 1) * TCW],
                        start=st, stop=sp), reads=[f"sqc{j}", "ones_c"],
                        writes=["pm21"])
            R = rows2
            p.op(V, lambda e: e.tensor_scalar(
                R[0:1, 0:TCW], pm2[0][0:1, 0:TCW], 1.0 / D, None, ALU.mult),
                reads=["pm20"], writes=["rows"])
            p.op(V, lambda e: e.tensor_scalar(
                R[0:1, TCW:2 * TCW], pm2[1][0:1, 0:TCW], 1.0 / D, None, ALU.mult),
                reads=["pm21"], writes=["rows"])
            p.op(V, lambda e: e.tensor_mul(
                R[0:1, 2 * TCW:3 * TCW], R[0:1, 0:TCW], R[0:1, 0:TCW]),
                reads=["rows"], writes=["rows"])
            p.op(V, lambda e: e.tensor_sub(
                R[0:1, 2 * TCW:3 * TCW], R[0:1, TCW:2 * TCW], R[0:1, 2 * TCW:3 * TCW]),
                reads=["rows"], writes=["rows"])
            p.op(V, lambda e: e.tensor_scalar(
                R[0:1, 2 * TCW:3 * TCW], R[0:1, 2 * TCW:3 * TCW], EPS, None,
                ALU.add), reads=["rows"], writes=["rows"])
            p.op(S, lambda e: e.activation(
                R[0:1, TCW:2 * TCW], R[0:1, 2 * TCW:3 * TCW], AF.Ln),
                reads=["rows"], writes=["rows"])
            p.op(S, lambda e: e.activation(
                R[0:1, 3 * TCW:4 * TCW], R[0:1, TCW:2 * TCW], AF.Exp, scale=-0.5),
                reads=["rows"], writes=["rows"])
            p.op(V, lambda e, tc=tc: e.tensor_scalar(
                negm2[0:1, tc * TCW:(tc + 1) * TCW], R[0:1, 0:TCW], -1.0, None,
                ALU.mult), reads=["rows"], writes=["negm2"])
            p.op(TE, lambda e, tc=tc: e.matmul(
                pm2[0][:, 0:TCW], ones_r_f[:, :], R[0:1, 3 * TCW:4 * TCW],
                start=True, stop=True), reads=["rows", "ones_r"], writes=["pm20"])
            p.op(V, lambda e, tc=tc: e.tensor_copy(
                rbc2[:, tc * TCW:(tc + 1) * TCW], pm2[0][:, 0:TCW]),
                reads=["pm20"], writes=["rbc2"])

          for erc in range(2):
            for tcp in range(NT2):
                ppar = (erc * NT2 + tcp) % 2
                PK = f"pm2{ppar}"
                for dc in range(DC):
                    for tw in range(NW2):
                        p.op(TE, lambda e, dc=dc, erc=erc, tcp=tcp, tw=tw, ppar=ppar:
                             e.matmul(
                                 pm2[ppar][:, tw * PW: tw * PW + TCW],
                                 A_sb[:, dc * 256 + erc * 128:
                                         dc * 256 + erc * 128 + 128],
                                 BB[:, dc * OWNC + tcp * W2 + tw * TCW:
                                       dc * OWNC + tcp * W2 + (tw + 1) * TCW],
                                 start=(dc == 0), stop=False),
                             reads=["A", "BBo", "BBp"], writes=[PK])
                for tw in range(NW2):
                    p.op(TE, lambda e, erc=erc, tcp=tcp, tw=tw, ppar=ppar: e.matmul(
                        pm2[ppar][:, tw * PW: tw * PW + TCW],
                        acol_bf[0:1, erc * 128:(erc + 1) * 128],
                        negm2[0:1, tcp * W2 + tw * TCW: tcp * W2 + (tw + 1) * TCW],
                        start=False, stop=True),
                        reads=["acol", "negm2"], writes=[PK])
                for tw in range(NW2):
                    p.op(V, lambda e, erc=erc, tcp=tcp, ppar=ppar, tw=tw:
                         e.tensor_tensor(
                             xa_sb[:, erc * OWNC + tcp * W2 + tw * TCW:
                                      erc * OWNC + tcp * W2 + (tw + 1) * TCW],
                             pm2[ppar][:, tw * PW: tw * PW + TCW],
                             rbc2[:, tcp * W2 + tw * TCW: tcp * W2 + (tw + 1) * TCW],
                             ALU.mult),
                         reads=[PK, "rbc2"], writes=["xa"])

          for oc in range(DC):
            wpar = oc % 2
            WK = f"wco{wpar}"
            p.op(G, lambda e, oc=oc, wpar=wpar: e.dma_start(
                out=wco[wpar].ap().rearrange("p (dc f) -> p dc f", f=128),
                in_=WT_blk[oc, :, :, :]),
                writes=[WK], dma=True, stream=f"wc{wpar}")
            for dc in range(DC):
                p.op(TE, lambda e, dc=dc, wpar=wpar: e.matmul(
                    ptb[0][0:1, 0:128], ones_c_bf[:, :],
                    wco[wpar][:, dc * 128:(dc + 1) * 128],
                    start=(dc == 0), stop=(dc == DC - 1)),
                    reads=[WK, "ones_c"], writes=["ptb0"])
            p.op(V, lambda e: e.tensor_copy(wcs2[0:1, :], ptb[0][0:1, 0:128]),
                 reads=["ptb0"], writes=["wcs2"])
            for tcp in range(NT2):
                ppar = tcp % 2
                PK = f"pm2{ppar}"
                for dc in range(DC):
                    for tw in range(NW2):
                        p.op(TE, lambda e, dc=dc, tcp=tcp, tw=tw, wpar=wpar, ppar=ppar:
                             e.matmul(
                                 pm2[ppar][:, tw * PW: tw * PW + TCW],
                                 wco[wpar][:, dc * 128:(dc + 1) * 128],
                                 BB[:, dc * OWNC + tcp * W2 + tw * TCW:
                                       dc * OWNC + tcp * W2 + (tw + 1) * TCW],
                                 start=(dc == 0), stop=False),
                             reads=[WK, "BBo", "BBp"], writes=[PK])
                for tw in range(NW2):
                    p.op(TE, lambda e, tcp=tcp, tw=tw, ppar=ppar: e.matmul(
                        pm2[ppar][:, tw * PW: tw * PW + TCW],
                        wcs2[0:1, :],
                        negm2[0:1, tcp * W2 + tw * TCW: tcp * W2 + (tw + 1) * TCW],
                        start=False, stop=True),
                        reads=["wcs2", "negm2"], writes=[PK])
                for tw in range(NW2):
                    p.op(V, lambda e, tcp=tcp, ppar=ppar, tw=tw: e.tensor_tensor(
                        tmp2[ppar][:, tw * TCW:(tw + 1) * TCW],
                        pm2[ppar][:, tw * PW: tw * PW + TCW],
                        rbc2[:, tcp * W2 + tw * TCW: tcp * W2 + (tw + 1) * TCW],
                        ALU.mult),
                        reads=[PK, "rbc2"], writes=[f"tmp{ppar}"])
                for tw in range(NW2):
                    for erc in range(2):
                        p.op(TE, lambda e, oc=oc, tcp=tcp, tw=tw, erc=erc, ppar=ppar:
                             e.matmul(
                                 pm2[ppar][:, tw * PW: tw * PW + TCW],
                                 Bfw_sb[:, erc * D + oc * 128:
                                           erc * D + (oc + 1) * 128],
                                 xa_sb[:, erc * OWNC + tcp * W2 + tw * TCW:
                                          erc * OWNC + tcp * W2 + (tw + 1) * TCW],
                                 start=(erc == 0), stop=False),
                             reads=[f"Bfw{erc}", "xa", f"tmp{ppar}"], writes=[PK])
                    p.op(TE, lambda e, oc=oc, tw=tw, ppar=ppar: e.matmul(
                        pm2[ppar][:, tw * PW: tw * PW + TCW],
                        b2_bf[0:1, oc * 128:(oc + 1) * 128],
                        ones_row_bf[0:1, :], start=False, stop=True),
                        reads=["b2", "ones_row"], writes=[PK])
                p.op(S, lambda e, oc=oc, tcp=tcp, ppar=ppar, TB=TB: e.dma_start(
                    out=xo1[ppar][:, :],
                    in_=x_fm_own[oc * 128:(oc + 1) * 128,
                                 TB + tcp * W2: TB + (tcp + 1) * W2]),
                    writes=[f"xo{ppar}"], dma=True, stream=f"xol{ppar}")
                p.op(S, lambda e, oc=oc, tcp=tcp, ppar=ppar, TB=TB: e.dma_start(
                    out=a1c[ppar][:, :],
                    in_=attn1_d[oc * 128:(oc + 1) * 128,
                                TB + tcp * W2: TB + (tcp + 1) * W2]),
                    reads=["attn1"], writes=[f"a1c{ppar}"], dma=True,
                    stream=f"a1l{ppar}")
                for tw in range(NW2):
                    p.op(V, lambda e, ppar=ppar, tw=tw: e.tensor_tensor(
                        fin[ppar][:, tw * TCW:(tw + 1) * TCW],
                        pm2[ppar][:, tw * PW: tw * PW + TCW],
                        tmp2[ppar][:, tw * TCW:(tw + 1) * TCW], ALU.add),
                        reads=[PK, f"tmp{ppar}"], writes=[f"fin{ppar}"])
                p.op(V, lambda e, ppar=ppar: e.tensor_tensor(
                    fin[ppar][:, :], fin[ppar][:, :], xo1[ppar][:, :], ALU.add),
                    reads=[f"fin{ppar}", f"xo{ppar}"], writes=[f"fin{ppar}"])
                p.op(V, lambda e, ppar=ppar: e.tensor_tensor(
                    fin[ppar][:, :], fin[ppar][:, :], a1c[ppar][:, :], ALU.add),
                    reads=[f"fin{ppar}", f"a1c{ppar}"], writes=[f"fin{ppar}"])
                NTT = W2 // 128
                nb = (NTT + 3) // 4
                for half in range(nb):
                    n_in_b = min(4, NTT - half * 4)
                    tpp = half % 2
                    for i in range(n_in_b):
                        tt = half * 4 + i
                        p.op(TE, lambda e, tt=tt, i=i, ppar=ppar, tpp=tpp: e.transpose(
                            ptb[tpp][:, i * 128:(i + 1) * 128],
                            fin[ppar][:, tt * 128:(tt + 1) * 128], ident[:, :]),
                            reads=[f"fin{ppar}", "ident"], writes=[f"ptb{tpp}"])
                    p.op(V, lambda e, tpp=tpp, n_in_b=n_in_b: e.tensor_copy(
                        tb[tpp][:, 0:n_in_b * 128], ptb[tpp][:, 0:n_in_b * 128]),
                        reads=[f"ptb{tpp}"], writes=[f"tb{tpp}"])
                    p.op(SY, lambda e, oc=oc, tcp=tcp, half=half, tpp=tpp,
                         n_in_b=n_in_b, TB=TB: e.dma_start(
                             out=out_e[TB + tcp * W2 + half * 512:
                                       TB + tcp * W2 + half * 512 + n_in_b * 128,
                                       oc * 128:(oc + 1) * 128].rearrange(
                                           "(n p) f -> p n f", p=128),
                             in_=tb[tpp].ap().rearrange(
                                 "p (n f) -> p n f", f=128)[:, 0:n_in_b, :]),
                         reads=[f"tb{tpp}"], writes=["out"], dma=True,
                         stream=f"ow{tpp}")
    p.barrier()
    p.replay()
    top.close()
    return nc, dbg_t


def wcol_ap(wco, dc):
    return wco[:, dc * 128:(dc + 1) * 128]


from concourse.bass_utils import run_bass_kernel_spmd

_CACHE = {}


def kernel(L0_attn_x, L0_attn_w_qkv, L0_attn_b_qkv, L0_attn_w_out, L0_attn_b_out,
           L0_ffn_W, L0_ffn_bias, L0_ffn_A, L0_ffn_Bf, L0_ffn_coeffs, L0_ffn_scales):
    cfg = CFG()
    in_maps = host_prepare(
        cfg, L0_attn_x, L0_attn_w_qkv, L0_attn_b_qkv, L0_attn_w_out, L0_attn_b_out,
        L0_ffn_W, L0_ffn_bias, L0_ffn_A, L0_ffn_Bf, L0_ffn_coeffs, L0_ffn_scales)
    if "nc" not in _CACHE:
        _CACHE["nc"] = build_nc(cfg)[0]
    nc = _CACHE["nc"]
    res = run_bass_kernel_spmd(nc, in_maps, core_ids=list(range(8)))
    out = np.empty((4, cfg.T, cfg.D), np.float32)
    for c in range(8):
        b, me = c // 2, c % 2
        out[b, cfg.OWN * me: cfg.OWN * (me + 1)] = res.results[c]["out_e"]
    return out
